# version 39
# baseline (speedup 1.0000x reference)
import numpy as np
import ml_dtypes
from contextlib import ExitStack

import concourse.bass as bass
import concourse.mybir as mybir
from concourse.bass_utils import run_bass_kernel_spmd

F32 = mybir.dt.float32
BF16 = mybir.dt.bfloat16
I32 = mybir.dt.int32
AF = mybir.ActivationFunctionType
ALU = mybir.AluOpType
BF = ml_dtypes.bfloat16
MAGIC = 0x5F3759DF

NCORES = 8
T = 1024
NT = 8
D = 4096
DG = 32
DIM = 512
G4 = 4
DFF = 2048
FG = 16
DEPTH = 4
H = 8
EPS = 1e-5

ENGINES = ("sync", "vector", "scalar", "gpsimd", "tensor")
SEMS = ("svec", "sact", "spool", "spe", "ssync", "sdma_sync", "sdma_gp", "sdma_act", "scc")


class Reg:
    __slots__ = ("name", "last_write", "readers")

    def __init__(self, name):
        self.name = name
        self.last_write = None
        self.readers = []


class Emitter:
    def __init__(self):
        self.ops = {e: [] for e in ENGINES}
        self.counts = {s: 0 for s in SEMS}
        self.watermark = {e: {} for e in ENGINES}

    def eng_sem(self, engine, dma):
        if dma:
            return {"sync": "sdma_sync", "gpsimd": "sdma_gp", "scalar": "sdma_act"}[engine]
        return {"vector": "svec", "scalar": "sact", "gpsimd": "spool",
                "tensor": "spe", "sync": "ssync"}[engine]

    def op(self, engine, fn, reads=(), writes=(), dma=False, cc=False, inc=None,
           no_selfdep=False):
        sem = "scc" if cc else self.eng_sem(engine, dma)
        if inc is None:
            inc = 16 if dma else 1
        needs = {}

        def need(dep):
            if dep is None:
                return
            s, c = dep
            if s == sem and no_selfdep:
                return
            if needs.get(s, 0) < c:
                needs[s] = c

        for r in reads:
            need(r.last_write)
        for w in writes:
            need(w.last_write)
            for rd in w.readers:
                need(rd)
        waits = []
        wm = self.watermark[engine]
        for s, c in sorted(needs.items()):
            if wm.get(s, 0) >= c:
                continue
            waits.append((s, c))
            wm[s] = c
        self.counts[sem] += inc
        cnt = self.counts[sem]
        self.ops[engine].append((waits, fn, sem, inc))
        for w in writes:
            w.last_write = (sem, cnt)
            w.readers = []
        for r in reads:
            r.readers.append((sem, cnt))
        return (sem, cnt)

    def barrier_wait(self, engine, dep):
        s, c = dep
        wm = self.watermark[engine]
        if wm.get(s, 0) >= c:
            return
        wm[s] = c
        self.ops[engine].append(([(s, c)], None, None, 0))

    def milestone(self, sem):
        return (sem, self.counts[sem])

    def full_barrier(self):
        ms = [(s, self.counts[s]) for s in SEMS if self.counts[s] > 0]
        for e in ENGINES:
            for m in ms:
                self.barrier_wait(e, m)

    def replay(self, block, sems):
        for ename in ENGINES:
            oplist = self.ops[ename]
            deco = getattr(block, ename)

            def make(oplist):
                def body(eng):
                    for waits, fn, sem, inc in oplist:
                        for s, c in waits:
                            eng.wait_ge(sems[s], c)
                        if fn is not None:
                            instr = fn(eng)
                            if sem is not None:
                                instr.then_inc(sems[sem], inc)
                return body
            deco(make(oplist))


def build_program(taps=()):
    taps = set(taps)
    nc = bass.Bass("TRN2", target_bir_lowering=False)
    es = ExitStack()

    def sb(name, shape, dtype):
        return es.enter_context(nc.sbuf_tensor(name, shape, dtype))

    def psum_t(name, shape, dtype=F32):
        return es.enter_context(nc.psum_tensor(name, shape, dtype))

    def dram_in(name, shape, dtype=F32):
        return nc.dram_tensor(name, shape, dtype, kind="ExternalInput").ap()

    x_d = dram_in("x", [T, D])
    winT_d = dram_in("winT", [D, DIM])
    bin_d = dram_in("bin", [1, DIM])
    w1T_d = dram_in("w1T", [DEPTH, DIM, DFF])
    w2T_d = dram_in("w2T", [DEPTH, DFF, DIM])
    b1_d = dram_in("b1c", [DEPTH, 128, FG])
    b2_d = dram_in("b2", [DEPTH, 1, DIM])
    lnv_d = {}
    for nm in ("ln1w", "ln1b", "ln2w", "ln2b"):
        lnv_d[nm] = dram_in(nm, [DEPTH, 1, DIM])
    phzre_d = dram_in("phzre", [DEPTH, 128, G4])
    phzim_d = dram_in("phzim", [DEPTH, 128, G4])
    lcre_d = dram_in("lcre", [DEPTH, 128, G4])
    lcim_d = dram_in("lcim", [DEPTH, 128, G4])
    scl_d = dram_in("sclc", [DEPTH, 128, G4])
    lnlw_d = dram_in("lnlw", [1, DIM])
    lnlb_d = dram_in("lnlb", [1, DIM])
    woutT_d = dram_in("woutT", [DIM, D])
    bout_d = dram_in("bout", [1, D])
    awrA_d = dram_in("awrA", [D, 128])
    awrW_d = dram_in("awrW", [D, H])
    bflat_d = dram_in("bflat", [128, D])
    br_d = dram_in("br", [1, H])
    eye_d = dram_in("eye", [128, 128])
    csel_d = dram_in("csel", [1, NCORES])

    out_d = nc.dram_tensor("out", [T, D], F32, kind="ExternalOutput").ap()

    tap_d = {}
    def deftap(name, shape):
        if name in taps:
            tap_d[name] = nc.dram_tensor("tap_" + name, shape, F32,
                                         kind="ExternalOutput").ap()
    deftap("h0", [T, DIM])
    deftap("tabs", [128, 12288])
    deftap("ytap", [128, D])
    deftap("uv0", [128, 2048])
    deftap("convtap", [128, 4096])
    deftap("fin", [128, 120])
    deftap("pcoltap", [128, 64])
    deftap("ffdbg", [128, 448])
    deftap("lnldbg", [128, 1568])
    for k in range(DEPTH):
        deftap("h%d" % (k + 1), [T, DIM])
    deftap("hf", [T, DIM])
    deftap("attn", [T, D])
    deftap("x1", [T, D])
    deftap("ffn", [T, D])

    cc_in = [nc.dram_tensor("cc_in%d" % k, [1, 1024], F32).ap() for k in range(DEPTH)]
    cc_out = [nc.dram_tensor("cc_out%d" % k, [NCORES, 1024], F32).ap() for k in range(DEPTH)]

    h = [sb("h%d" % j, [128, DIM], F32) for j in range(NT)]
    xstage = [sb("xst%d" % i, [128, D], F32) for i in range(2)]
    xbuf16 = sb("xbuf16", [128, D], BF16)
    xT16 = [sb("xT16_%d" % i, [128, D], BF16) for i in range(2)]
    wbuf = sb("wbuf", [128, 16384], BF16)
    tabbuf = sb("tabbuf", [128, 12288], F32)
    scanbuf = sb("scanbuf", [128, 4096], F32)
    deadbuf = sb("deadbuf", [128, 512], F32)
    z1T = sb("z1T", [128, 8192], BF16)
    awrW16 = sb("awrW16", [128, DG * H], BF16)

    ones16 = sb("ones16", [1, 128], BF16)
    binr16 = sb("binr16", [1, DIM], BF16)
    b2r16 = sb("b2r16", [1, DIM], BF16)
    l1wb16 = sb("l1wb16", [128, DIM], BF16)
    l1bb16 = sb("l1bb16", [128, DIM], BF16)
    l2wb16 = sb("l2wb16", [128, DIM], BF16)
    l2bb16 = sb("l2bb16", [128, DIM], BF16)
    b2b = sb("b2b", [128, DIM], F32)
    b1c = sb("b1cs", [128, FG], F32)
    brb = sb("brb", [128, H], F32)
    eyef = sb("eyef", [128, 128], F32)
    eye16 = sb("eye16", [128, 128], BF16)
    scsel = sb("scsel", [128, NCORES], F32)

    pcol = sb("pcol", [128, 16 * G4], F32)
    dpre = sb("dpre", [128, G4 * 16], F32)
    dpim = sb("dpim", [128, G4 * 16], F32)
    w9re = sb("w9re", [128, 9 * G4], F32)
    w9im = sb("w9im", [128, 9 * G4], F32)
    t9a = sb("t9a", [128, G4 * 9], F32)
    t9b = sb("t9b", [128, G4 * 9], F32)
    gath = sb("gath", [128, 72], F32)
    gcol = sb("gcol", [128, 8 * G4], F32)
    fcol = sb("fcol", [128, 16], F32)
    gstage = sb("gstage", [128, 8], F32)
    gstage2 = sb("gstage2", [128, 8], F32)

    mvst = sb("mvst", [128, 16], F32)
    rstd8 = sb("rstd8", [128, NT], F32)
    nmr8 = sb("nmr8", [128, NT], F32)
    bnbuf = sb("bnbuf", [128, 48], F32)
    nsc = [sb("nsc%d" % i, [128, 8], F32) for i in range(4)]
    nsci = sb("nsci", [128, 8], I32)
    statc = sb("statc", [128, 8], F32)
    stat2 = sb("stat2", [128, 8], F32)
    tmp512a = sb("tmp512a", [128, DIM], F32)
    tmp512b = sb("tmp512b", [128, DIM], F32)
    tmp512c = sb("tmp512c", [128, DIM], F32)

    ps = [psum_t("ps%d" % i, [128, 512], F32) for i in range(8)]

    winT16 = wbuf
    w1T16 = wbuf[:, 0:G4 * DFF]
    w2T16 = wbuf[:, G4 * DFF:]
    woutT16 = wbuf
    ctab = tabbuf[:, 0:4096]
    stab = tabbuf[:, 4096:8192]
    p0tab = tabbuf[:, 8192:12288]
    c3 = ctab.rearrange("p (g t) -> p g t", t=1024)
    s3 = stab.rearrange("p (g t) -> p g t", t=1024)
    p3 = p0tab.rearrange("p (g t) -> p g t", t=1024)
    yT = xT16[0]
    convT = xT16[1]
    z1Tf = z1T[:].bitcast(F32)
    u2 = [scanbuf[:, 0:1024], scanbuf[:, 2048:3072],
          z1Tf[:, 0:1024], z1Tf[:, 2048:3072]]
    v2 = [scanbuf[:, 1024:2048], scanbuf[:, 3072:4096],
          z1Tf[:, 1024:2048], z1Tf[:, 3072:4096]]
    cv16 = [xbuf16[:, 0:1024], xbuf16[:, 1024:2048],
            xbuf16[:, 2048:3072], xbuf16[:, 3072:4096]]
    tg1 = xstage[0][:, 0:2048]
    tg2 = xstage[0][:, 2048:4096]
    tg1v = tg1.rearrange("p (g t) -> p g t", t=512)
    tg2v = tg2.rearrange("p (g t) -> p g t", t=512)
    big = [tabbuf[:, 0:4096], tabbuf[:, 4096:8192]]
    b16c = tabbuf[:, 8192:10240].bitcast(BF16)
    awrA16 = tabbuf[:, 10240:12288].bitcast(BF16)
    boutb = scanbuf
    exps = deadbuf[:, 0:8]
    logitsb = deadbuf[:, 8:16]
    sinvc = deadbuf[:, 16:17]
    ml16 = deadbuf[:, 256:320].bitcast(BF16)
    mlT16 = deadbuf[:, 320:384].bitcast(BF16)
    hfT = yT
    x1lnT = convT

    em = Emitter()
    R = {}
    def reg(name):
        if name not in R:
            R[name] = Reg(name)
        return R[name]

    def E(engine, fn, reads=(), writes=(), **kw):
        em.op(engine, fn, reads=[reg(r) for r in reads],
              writes=[reg(w) for w in writes], **kw)

    WBUF = ["wbufA", "wbufB"]
    TAB = ["ctabA", "ctabB", "stabA", "stabB", "p0tab"]
    XQ = ["xq0", "xq1", "xq2", "xq3"]
    SCN = ["scn0", "scn1", "scn2", "scn3"]
    UVR = [("scn0", "scn1"), ("scn2", "scn3"), ("z1q0", "z1q1"), ("z1q2", "z1q3")]
    Z1R = ["z1q0", "z1q1", "z1q2", "z1q3"]
    XS0 = ["xst0a", "xst0b"]

    E("gpsimd", lambda g: g.dma_start(out=eyef[:], in_=eye_d[:]), writes=["eyef"], dma=True)
    E("gpsimd", lambda g: g.dma_start(out=scsel[:], in_=csel_d.partition_broadcast(128)),
      writes=["scsel"], dma=True)
    E("gpsimd", lambda g: g.dma_start(
        out=winT16[:].rearrange("p (g d) -> p g d", d=DIM),
        in_=winT_d.rearrange("(g p) d -> p g d", p=128)), writes=WBUF, dma=True)
    E("gpsimd", lambda g: g.dma_start(out=brb[:], in_=br_d.partition_broadcast(128)),
      writes=["brb"], dma=True)
    E("vector", lambda v: v.tensor_copy(eye16[:], eyef[:]), ["eyef"], ["eye16"])
    E("vector", lambda v: v.memset(ones16[:], 1.0), [], ["ones16"])
    E("vector", lambda v: v.memset(deadbuf[:, 17:18], float(0.25 * 2.0 * np.pi * (1.0 - 3e-7))),
      [], ["qbias"])
    E("gpsimd", lambda g: g.dma_start(out=binr16[:], in_=bin_d[:]), writes=["binr16"], dma=True)

    def newton_rsqrt(eng, out_ap, out_w, var_ap, var_reads, width, add_eps):
        a, bb, y, t = (nsc[0][:, 0:width], nsc[1][:, 0:width],
                       nsc[2][:, 0:width], nsc[3][:, 0:width])
        i_ = nsci[:, 0:width]
        if add_eps:
            E(eng, lambda g: g.tensor_scalar(a, var_ap, 1.0, EPS, ALU.mult, ALU.add),
              var_reads, ["nsc0"])
        else:
            E(eng, lambda g: g.tensor_copy(a, var_ap), var_reads, ["nsc0"])
        E(eng, lambda g: g.tensor_scalar(i_, a.bitcast(I32), 1, None, ALU.logical_shift_right),
          ["nsc0"], ["nsci"])
        E(eng, lambda g: g.tensor_scalar(i_, i_, 0xFFFFFFFF, None, ALU.bitwise_xor),
          ["nsci"], ["nsci"])
        E(eng, lambda g: g.tensor_scalar(i_, i_, MAGIC + 1, None, ALU.add),
          ["nsci"], ["nsci"])
        E(eng, lambda g: g.tensor_copy(y, i_.bitcast(F32)), ["nsci"], ["nsc2"])
        E(eng, lambda g: g.tensor_scalar(bb, a, 0.5, None, ALU.mult), ["nsc0"], ["nsc1"])
        for _ in range(3):
            E(eng, lambda g: g.tensor_tensor(t, y, y, ALU.mult), ["nsc2"], ["nsc3"])
            E(eng, lambda g: g.tensor_tensor(t, t, bb, ALU.mult), ["nsc3", "nsc1"], ["nsc3"])
            E(eng, lambda g: g.tensor_scalar(t, t, -1.0, 1.5, ALU.mult, ALU.add),
              ["nsc3"], ["nsc3"])
            E(eng, lambda g: g.tensor_tensor(y, y, t, ALU.mult), ["nsc2", "nsc3"], ["nsc2"])
        E(eng, lambda g: g.tensor_copy(out_ap, y), ["nsc2"], [out_w])

    def ln_stats(src_ap, src_regs, j, chunks):
        for cch in range(chunks):
            E("vector", lambda v, cch=cch: v.bn_stats(
                  bnbuf[:, cch*6:(cch+1)*6], src_ap[:, cch*512:(cch+1)*512]),
              src_regs, ["bnbuf"])
        E("vector", lambda v, j=j, cch=chunks: v.bn_aggr(mvst[:, 2*j:2*j+2],
                                                         bnbuf[:, 0:6*cch]),
          ["bnbuf"], ["mvst"])

    def stats_from_sums(j, nelem, ncol=8):
        if ncol > 1:
            E("vector", lambda v, j=j, nc=ncol: v.tensor_reduce(
                  statc[:, 0:1], statc[:, 0:nc], mybir.AxisListType.X, ALU.add),
              ["statc"], ["statc"])
        E("vector", lambda v, j=j, ne=nelem: v.tensor_scalar(mvst[:, 2*j:2*j+1], statc[:, 0:1],
                                                             1.0/ne, None, ALU.mult),
          ["statc"], ["mvst"])
        E("vector", lambda v, j=j: v.tensor_tensor(statc[:, 1:2], mvst[:, 2*j:2*j+1],
                                                   mvst[:, 2*j:2*j+1], ALU.mult),
          ["mvst"], ["statc"])
        E("vector", lambda v, j=j, ne=nelem: v.scalar_tensor_tensor(
              mvst[:, 2*j+1:2*j+2], stat2[:, 0:1], 1.0/ne, statc[:, 1:2],
              ALU.mult, ALU.subtract),
          ["stat2", "statc"], ["mvst"])

    def finalize_batch(j0, n, eng):
        var_ap = mvst[:, 2*j0+1: 2*(j0+n): 2]
        mu_ap = mvst[:, 2*j0: 2*(j0+n): 2]
        newton_rsqrt(eng, rstd8[:, j0:j0+n], "rstd8", var_ap, ["mvst"], n, True)
        E(eng, lambda g: g.tensor_tensor(nmr8[:, j0:j0+n], mu_ap,
                                         rstd8[:, j0:j0+n], ALU.mult),
          ["mvst", "rstd8"], ["nmr8"])
        E(eng, lambda g: g.tensor_scalar(nmr8[:, j0:j0+n], nmr8[:, j0:j0+n], -1.0,
                                         None, ALU.mult),
          ["nmr8"], ["nmr8"])

    pendingA = []
    def load_x(j):
        sl = j % 2
        xsj = xstage[sl]
        xr = XS0 if sl == 0 else ["xst1"]
        E("sync", lambda s, j=j, xsj=xsj: s.dma_start(out=xsj[:], in_=x_d[j*128:(j+1)*128, :]),
          writes=xr, dma=True)
    load_x(0)
    for j in range(NT):
        sl = j % 2
        xs = xstage[sl]
        xsregs = XS0 if sl == 0 else ["xst1"]
        if j + 1 < NT:
            load_x(j + 1)
        ln_stats(xs[:], xsregs, j, 8)
        finalize_batch(j, 1, "vector")
        E("scalar", lambda s, j=j, xs=xs: s.activation(
              out=xbuf16[:], in_=xs[:], func=AF.Identity,
              scale=rstd8[:, j:j+1], bias=nmr8[:, j:j+1]),
          xsregs + ["rstd8", "nmr8"], XQ)
        xt = xT16[sl]
        rxt = ["yTa", "yTb"] if sl == 0 else ["cTbuf"]
        E("sync", lambda s, xt=xt: s.dma_start(
              out=xt[:].rearrange("p (g t) -> p g t", g=DG), in_=xbuf16[:], transpose=True),
          XQ, rxt, dma=True)
        rp = "ps%d" % (j % 2)
        for g in range(DG):
            E("tensor", lambda t, g=g, xt=xt, pj=ps[j % 2]: t.matmul(
                  pj[:], xt[:, g*128:(g+1)*128], winT16[:, g*DIM:(g+1)*DIM],
                  start=(g == 0), stop=False),
              rxt + WBUF, [rp], no_selfdep=True)
        E("tensor", lambda t, pj=ps[j % 2]: t.matmul(
              pj[:], ones16[:], binr16[:], start=False, stop=True),
          ["ones16", "binr16"], [rp], no_selfdep=True)
        pendingA.append((j, rp))
        if len(pendingA) >= 2:
            jj2, rp2 = pendingA.pop(0)
            E("scalar", lambda s, jj2=jj2, pj=ps[jj2 % 2]: s.activation(
                  out=h[jj2][:], in_=pj[:], func=AF.Identity),
              [rp2], ["h%d" % jj2])
    while pendingA:
        jj2, rp2 = pendingA.pop(0)
        E("scalar", lambda s, jj2=jj2, pj=ps[jj2 % 2]: s.activation(
              out=h[jj2][:], in_=pj[:], func=AF.Identity),
          [rp2], ["h%d" % jj2])

    if "h0" in taps:
        for j in range(NT):
            E("sync", lambda s, j=j: s.dma_start(
                  out=tap_d["h0"][j*128:(j+1)*128, :], in_=h[j][:]),
              ["h%d" % j], dma=True)

    PR, PI, LR, LI, SC, CO, SI, RH, R2, S1, S2, S3 = range(12)
    def pc(i):
        return pcol[:, i*G4:(i+1)*G4]

    def load_phz(k):
        for slot, d in ((PR, phzre_d), (PI, phzim_d), (LR, lcre_d), (LI, lcim_d),
                        (SC, scl_d)):
            E("gpsimd", lambda g, slot=slot, d=d, k=k: g.dma_start(out=pc(slot), in_=d[k]),
              writes=["pcol"], dma=True)

    def load_ln_vectors(k):
        E("gpsimd", lambda g, k=k: g.dma_start(out=l1wb16[:], in_=lnv_d["ln1w"][k].partition_broadcast(128)), writes=["l1wb"], dma=True)
        E("gpsimd", lambda g, k=k: g.dma_start(out=l1bb16[:], in_=lnv_d["ln1b"][k].partition_broadcast(128)), writes=["l1bb"], dma=True)
        E("gpsimd", lambda g, k=k: g.dma_start(out=l2wb16[:], in_=lnv_d["ln2w"][k].partition_broadcast(128)), writes=["l2wb"], dma=True)
        E("gpsimd", lambda g, k=k: g.dma_start(out=l2bb16[:], in_=lnv_d["ln2b"][k].partition_broadcast(128)), writes=["l2bb"], dma=True)

    def load_b_vectors(k):
        E("gpsimd", lambda g, k=k: g.dma_start(out=b2b[:], in_=b2_d[k].partition_broadcast(128)), writes=["b2b"], dma=True)
        E("gpsimd", lambda g, k=k: g.dma_start(out=b1c[:], in_=b1_d[k]), writes=["b1c"], dma=True)

    def load_w1(k):
        E("gpsimd", lambda g, k=k: g.dma_start(
              out=w1T16.rearrange("p (g f) -> p g f", f=DFF),
              in_=w1T_d[k].rearrange("(g p) f -> p g f", p=128)),
          writes=["wbufA"], dma=True)

    def load_w2(k):
        E("gpsimd", lambda g, k=k: g.dma_start(
              out=w2T16.rearrange("p (g d) -> p g d", d=DIM),
              in_=w2T_d[k].rearrange("(g p) d -> p g d", p=128)),
          writes=["wbufB"], dma=True)

    TWOPI_S = float(2.0 * np.pi * (1.0 - 3e-7))
    INV2PI = float(1.0 / (2.0 * np.pi))

    def gen_tables_a():
        T2P, LNR, MSK, SGS = 12, 13, 14, 15
        E("vector", lambda v: v.tensor_tensor(pc(R2), pc(PR), pc(PR), ALU.mult),
          ["pcol"], ["pcol"])
        E("vector", lambda v: v.scalar_tensor_tensor(pc(S1), pc(PI), 1.0, pc(PI), ALU.mult, ALU.mult),
          ["pcol"], ["pcol"])
        E("vector", lambda v: v.tensor_tensor(pc(R2), pc(R2), pc(S1), ALU.add),
          ["pcol"], ["pcol"])
        E("vector", lambda v: v.tensor_scalar(pc(R2), pc(R2), 1.0, 1e-30, ALU.mult, ALU.add),
          ["pcol"], ["pcol"])
        newton_rsqrt("vector", pc(S1), "pcol", pc(R2), ["pcol"], G4, False)
        E("vector", lambda v: v.tensor_tensor(pc(S2), pc(R2), pc(S1), ALU.mult),
          ["pcol"], ["pcol"])
        E("vector", lambda v: v.tensor_tensor(pc(CO), pc(PR), pc(S1), ALU.mult),
          ["pcol"], ["pcol"])
        E("vector", lambda v: v.tensor_tensor(pc(SI), pc(PI), pc(S1), ALU.mult),
          ["pcol"], ["pcol"])
        E("scalar", lambda s: s.activation(out=pc(RH), in_=pc(S2), func=AF.Tanh),
          ["pcol"], ["pcol"])
        E("vector", lambda v: v.tensor_scalar(pc(R2), pc(CO), 1.0, 1.0, ALU.mult, ALU.add),
          ["pcol"], ["pcol"])
        E("vector", lambda v: v.tensor_scalar(pc(S1), pc(CO), -1.0, 1.0, ALU.mult, ALU.add),
          ["pcol"], ["pcol"])
        E("vector", lambda v: v.tensor_scalar(pc(R2), pc(R2), 1.0, None, ALU.max),
          ["pcol"], ["pcol"])
        E("vector", lambda v: v.tensor_scalar(pc(S1), pc(S1), 1.0, None, ALU.max),
          ["pcol"], ["pcol"])
        E("vector", lambda v: v.reciprocal(pcol[:, R2*G4:R2*G4+2*G4],
                                           pcol[:, R2*G4:R2*G4+2*G4]),
          ["pcol"], ["pcol"])
        E("vector", lambda v: v.tensor_scalar(pc(MSK), pc(CO), 0.0, None, ALU.is_ge),
          ["pcol"], ["pcol"])
        E("vector", lambda v: v.tensor_tensor(pc(S2), pc(SI), pc(R2), ALU.mult),
          ["pcol"], ["pcol"])
        E("vector", lambda v: v.tensor_tensor(pc(S3), pc(SI), pc(S1), ALU.mult),
          ["pcol"], ["pcol"])
        E("vector", lambda v: v.tensor_tensor(pc(S2), pc(S2), pc(S3), ALU.subtract),
          ["pcol"], ["pcol"])
        E("vector", lambda v: v.tensor_tensor(pc(S2), pc(S2), pc(MSK), ALU.mult),
          ["pcol"], ["pcol"])
        E("vector", lambda v: v.tensor_tensor(pc(S2), pc(S2), pc(S3), ALU.add),
          ["pcol"], ["pcol"])
        E("scalar", lambda s: s.activation(out=pc(S2), in_=pc(S2), func=AF.Arctan),
          ["pcol"], ["pcol"])
        E("vector", lambda v: v.tensor_scalar(pc(SGS), pc(SI), 0.0, None, ALU.is_lt),
          ["pcol"], ["pcol"])
        E("vector", lambda v: v.tensor_scalar(pc(R2), pc(MSK), 4.0, -2.0, ALU.mult, ALU.add),
          ["pcol"], ["pcol"])
        E("vector", lambda v: v.tensor_tensor(pc(R2), pc(R2), pc(S2), ALU.mult),
          ["pcol"], ["pcol"])
        E("vector", lambda v: v.tensor_scalar(pc(S1), pc(SGS), float(-2.0*np.pi), float(np.pi),
                                              ALU.mult, ALU.add),
          ["pcol"], ["pcol"])
        E("vector", lambda v: v.tensor_scalar(pc(S3), pc(MSK), -1.0, 1.0, ALU.mult, ALU.add),
          ["pcol"], ["pcol"])
        E("vector", lambda v: v.tensor_tensor(pc(S1), pc(S1), pc(S3), ALU.mult),
          ["pcol"], ["pcol"])
        E("vector", lambda v: v.tensor_tensor(pc(T2P), pc(R2), pc(S1), ALU.add),
          ["pcol"], ["pcol"])
        E("vector", lambda v: v.tensor_scalar(pc(T2P), pc(T2P), INV2PI, None, ALU.mult),
          ["pcol"], ["pcol"])
        E("vector", lambda v: v.tensor_scalar(pc(SGS), pc(RH), -1.0, 1.0, ALU.mult, ALU.add),
          ["pcol"], ["pcol"])
        E("vector", lambda v: v.tensor_scalar(pc(S1), pc(SGS), 0.25, 1.0/3.0, ALU.mult, ALU.add),
          ["pcol"], ["pcol"])
        E("vector", lambda v: v.tensor_tensor(pc(S1), pc(S1), pc(SGS), ALU.mult),
          ["pcol"], ["pcol"])
        E("vector", lambda v: v.tensor_scalar(pc(S1), pc(S1), 1.0, 0.5, ALU.mult, ALU.add),
          ["pcol"], ["pcol"])
        E("vector", lambda v: v.tensor_tensor(pc(S1), pc(S1), pc(SGS), ALU.mult),
          ["pcol"], ["pcol"])
        E("vector", lambda v: v.tensor_scalar(pc(S1), pc(S1), 1.0, 1.0, ALU.mult, ALU.add),
          ["pcol"], ["pcol"])
        E("vector", lambda v: v.tensor_tensor(pc(S1), pc(S1), pc(SGS), ALU.mult),
          ["pcol"], ["pcol"])
        E("vector", lambda v: v.tensor_scalar(pc(S1), pc(S1), -1.0, None, ALU.mult),
          ["pcol"], ["pcol"])
        E("scalar", lambda s: s.activation(out=pc(LNR), in_=pc(RH), func=AF.Ln),
          ["pcol"], ["pcol"])
        E("vector", lambda v: v.tensor_scalar(pc(MSK), pc(SGS), 0.1, None, ALU.is_lt),
          ["pcol"], ["pcol"])
        E("vector", lambda v: v.tensor_tensor(pc(S1), pc(S1), pc(LNR), ALU.subtract),
          ["pcol"], ["pcol"])
        E("vector", lambda v: v.tensor_tensor(pc(S1), pc(S1), pc(MSK), ALU.mult),
          ["pcol"], ["pcol"])
        E("vector", lambda v: v.tensor_tensor(pc(LNR), pc(LNR), pc(S1), ALU.add),
          ["pcol"], ["pcol"])
        iotq = scanbuf[:, 0:1024]
        onesq = scanbuf[:, 1024:2048]
        q4 = xstage[0][:].rearrange("p (g t) -> p g t", t=1024)
        qp4 = xstage[1][:].rearrange("p (g t) -> p g t", t=1024)
        t2pb = pc(T2P).rearrange("p (g o) -> p g o", o=1).broadcast_to((128, G4, 1024))
        lnrb = pc(LNR).rearrange("p (g o) -> p g o", o=1).broadcast_to((128, G4, 1024))
        iotb = iotq.unsqueeze(1).broadcast_to((128, G4, 1024))
        ci32 = ctab.bitcast(I32)
        TABC = ["ctabA", "ctabB"]
        TABS = ["stabA", "stabB"]
        E("vector", lambda v: v.memset(onesq, 1.0), [], ["scn1"])
        E("vector", lambda v: v.tensor_tensor_scan(
              out=iotq, data0=onesq, data1=onesq, initial=-1.0, op0=ALU.mult, op1=ALU.add),
          ["scn1"], ["scn0"])
        E("vector", lambda v: v.tensor_tensor(q4, iotb, t2pb, ALU.mult),
          ["scn0", "pcol"], XS0)
        E("vector", lambda v: v.tensor_tensor(qp4, iotb, lnrb, ALU.mult),
          ["scn0", "pcol"], ["xst1"])
        E("gpsimd", lambda g: g.tensor_copy(ci32[:], xstage[0][:]), XS0, TABC)
        E("gpsimd", lambda g: g.tensor_copy(stab[:], ci32[:]), TABC, TABS)
        E("gpsimd", lambda g: g.tensor_tensor(ctab[:], xstage[0][:], stab[:], ALU.subtract),
          TABS + XS0, TABC)
        E("vector", lambda v: v.tensor_scalar(xstage[0][:], ctab[:], 0.25, None, ALU.is_ge),
          TABC, XS0)
        E("gpsimd", lambda g: g.tensor_tensor(xstage[0][:], ctab[:], xstage[0][:], ALU.subtract),
          XS0 + TABC, XS0)

    def gen_tables_b():
        TABC = ["ctabA", "ctabB"]
        TABS = ["stabA", "stabB"]
        E("scalar", lambda s: s.activation(out=p0tab[:], in_=xstage[1][:], func=AF.Exp),
          ["xst1"], ["p0tab"])
        E("scalar", lambda s: s.activation(out=stab[:], in_=ctab[:], func=AF.Sin, scale=TWOPI_S),
          TABC, TABS)
        E("scalar", lambda s: s.activation(out=ctab[:], in_=xstage[0][:], func=AF.Sin,
                                           scale=TWOPI_S, bias=deadbuf[:, 17:18]),
          XS0 + ["qbias"], TABC)
        c1023 = c3[:, :, 1023]
        s1023 = s3[:, :, 1023]
        p1023 = p3[:, :, 1023]
        TABR = ["ctabA", "ctabB", "stabA", "stabB"]
        E("gpsimd", lambda v: v.tensor_tensor(pc(S2), p1023, pc(RH), ALU.mult),
          ["p0tab", "pcol"], ["pcol"])
        E("gpsimd", lambda v: v.tensor_tensor(pc(S1), c1023, pc(CO), ALU.mult), TABR + ["pcol"], ["pcol"])
        E("gpsimd", lambda v: v.tensor_tensor(pc(S3), s1023, pc(SI), ALU.mult), TABR + ["pcol"], ["pcol"])
        E("gpsimd", lambda v: v.tensor_tensor(pc(S1), pc(S1), pc(S3), ALU.subtract), ["pcol"], ["pcol"])
        E("gpsimd", lambda v: v.tensor_tensor(pc(S3), s1023, pc(CO), ALU.mult), TABR + ["pcol"], ["pcol"])
        E("gpsimd", lambda v: v.tensor_tensor(pc(R2), c1023, pc(SI), ALU.mult), TABR + ["pcol"], ["pcol"])
        E("gpsimd", lambda v: v.tensor_tensor(pc(S3), pc(S3), pc(R2), ALU.add), ["pcol"], ["pcol"])
        dre = dpre[:].rearrange("p (g i) -> p g i", i=16)
        dim_ = dpim[:].rearrange("p (g i) -> p g i", i=16)
        E("gpsimd", lambda v: v.memset(dpre[:], 0.0), [], ["dpre"])
        E("gpsimd", lambda v: v.memset(dpim[:], 0.0), [], ["dpim"])
        E("gpsimd", lambda v: v.memset(dre[:, :, 8:9], 1.0), ["dpre"], ["dpre"])
        E("gpsimd", lambda v: v.tensor_tensor(gcol[:, 0:G4], pc(S2), pc(S1), ALU.mult),
          ["pcol"], ["gcolD"])
        E("gpsimd", lambda v: v.tensor_tensor(gcol[:, G4:2*G4], pc(S2), pc(S3), ALU.mult),
          ["pcol"], ["gcolD"])
        for kk2 in range(1, 8):
            prc, pic = dre[:, :, 7+kk2], dim_[:, :, 7+kk2]
            E("gpsimd", lambda v, prc=prc: v.tensor_tensor(t9a[:, 0:G4], prc, gcol[:, 0:G4], ALU.mult),
              ["dpre", "gcolD"], ["t9a"])
            E("gpsimd", lambda v, pic=pic: v.tensor_tensor(t9b[:, 0:G4], pic, gcol[:, G4:2*G4], ALU.mult),
              ["dpim", "gcolD"], ["t9b"])
            E("gpsimd", lambda v, kk2=kk2: v.tensor_tensor(dre[:, :, 8+kk2], t9a[:, 0:G4], t9b[:, 0:G4], ALU.subtract),
              ["t9a", "t9b"], ["dpre"])
            E("gpsimd", lambda v, prc=prc: v.tensor_tensor(t9a[:, 0:G4], prc, gcol[:, G4:2*G4], ALU.mult),
              ["dpre", "gcolD"], ["t9a"])
            E("gpsimd", lambda v, pic=pic: v.tensor_tensor(t9b[:, 0:G4], pic, gcol[:, 0:G4], ALU.mult),
              ["dpim", "gcolD"], ["t9b"])
            E("gpsimd", lambda v, kk2=kk2: v.tensor_tensor(dim_[:, :, 8+kk2], t9a[:, 0:G4], t9b[:, 0:G4], ALU.add),
              ["t9a", "t9b"], ["dpim"])

    load_w1(0)
    load_w2(0)
    load_phz(0)
    load_ln_vectors(0)
    load_b_vectors(0)
    gen_tables_a()
    gen_tables_b()
    if "pcoltap" in taps:
        E("sync", lambda s: s.dma_start(out=tap_d["pcoltap"][:, :], in_=pcol[:]),
          ["pcol"], dma=True)
    if "tabs" in taps:
        E("sync", lambda s: s.dma_start(out=tap_d["tabs"][:, :], in_=tabbuf[:]),
          ["ctabA", "ctabB", "stabA", "stabB", "p0tab"], dma=True)

    def emit_front(js):
        for j in js:
            ln_stats(h[j][:], ["h%d" % j], j, 1)
        finalize_batch(js[0], len(js), "vector")
        for j in js:
            q = j % 4
            y16 = xbuf16[:, q*1024:q*1024 + 512]
            rq = "xq%d" % q
            tmpx = tmp512a if j % 2 == 0 else tmp512b
            rt = "tmp512a" if j % 2 == 0 else "tmp512b"
            tmp16 = tmpx[:].bitcast(BF16)[:, 0:DIM]
            E("scalar", lambda s, j=j, tmp16=tmp16: s.activation(
                  out=tmp16, in_=h[j][:], func=AF.Identity,
                  scale=rstd8[:, j:j+1], bias=nmr8[:, j:j+1]),
              ["h%d" % j, "rstd8", "nmr8"], [rt])
            E("vector", lambda v, tmp16=tmp16: v.tensor_tensor(tmp16, tmp16, l1wb16[:], ALU.mult),
              [rt, "l1wb"], [rt])
            E("gpsimd", lambda v, tmp16=tmp16: v.tensor_tensor(tmp16, tmp16, l1bb16[:], ALU.add),
              [rt, "l1bb"], [rt])
            E("scalar", lambda s, y16=y16, tmp16=tmp16: s.activation(out=y16, in_=tmp16, func=AF.Silu),
              [rt], [rq])
            E("sync", lambda s, j=j, y16=y16: s.dma_start(
                  out=yT[:].rearrange("p (g t) -> p g t", t=1024)[:, :, j*128:(j+1)*128],
                  in_=y16, transpose=True),
              [rq], ["yTa" if j < 4 else "yTb"], dma=True)

    emit_front(list(range(NT)))

    for k in range(DEPTH):
        hreg = ["h%d" % j for j in range(NT)]
        if "ytap" in taps and k == 0:
            E("gpsimd", lambda s: s.dma_start(out=tap_d["ytap"][:, :], in_=yT[:]),
              ["yTa", "yTb"], dma=True)
        for g in range(G4):
            ug, vg = u2[g], v2[g]
            ru, rv = UVR[g]
            ytg = yT[:, g*1024:(g+1)*1024]
            E("vector", lambda v, g=g, ug=ug, ytg=ytg: v.tensor_tensor(
                  ug, c3[:, g, :], ytg, ALU.mult), ["ctabA" if g < 3 else "ctabB", "yTa", "yTb"], [ru])
            E("gpsimd", lambda v, g=g, vg=vg, ytg=ytg: v.tensor_tensor(
                  vg, s3[:, g, :], ytg, ALU.mult), ["stabA" if g < 3 else "stabB", "yTa", "yTb"], [rv])
            p0rev = p3[:, g, ::-1]
            E("vector", lambda v, ug=ug, p0rev=p0rev, g=g: v.scalar_tensor_tensor(
                  convT[:, 0:1024], ug, 1.0, p0rev, ALU.mult, ALU.mult,
                  accum_out=fcol[:, 2*g:2*g+1]),
              [ru, "p0tab"], ["cTbuf", "fcol"])
            E("vector", lambda v, vg=vg, p0rev=p0rev, g=g: v.scalar_tensor_tensor(
                  convT[:, 0:1024], vg, 1.0, p0rev, ALU.mult, ALU.mult,
                  accum_out=fcol[:, 2*g+1:2*g+2]),
              [rv, "p0tab"], ["cTbuf", "fcol"])

        c1023v = c3[:, :, 1023]
        s1023v = s3[:, :, 1023]
        ucols = fcol[:, 0:8:2]
        vcols = fcol[:, 1:8:2]
        E("vector", lambda v: v.tensor_tensor(gcol[:, 2*G4:3*G4], c1023v, ucols, ALU.mult),
          ["ctabA", "ctabB", "fcol"], ["gcolT"])
        E("vector", lambda v: v.tensor_tensor(gcol[:, 3*G4:4*G4], s1023v, vcols, ALU.mult),
          ["stabA", "stabB", "fcol"], ["gcolT"])
        E("vector", lambda v: v.tensor_tensor(gcol[:, 2*G4:3*G4], gcol[:, 2*G4:3*G4], gcol[:, 3*G4:4*G4], ALU.add),
          ["gcolT"], ["gcolT"])
        E("vector", lambda v: v.tensor_tensor(gcol[:, 4*G4:5*G4], s1023v, ucols, ALU.mult),
          ["stabA", "stabB", "fcol"], ["gcolT"])
        E("vector", lambda v: v.tensor_tensor(gcol[:, 5*G4:6*G4], c1023v, vcols, ALU.mult),
          ["ctabA", "ctabB", "fcol"], ["gcolT"])
        E("vector", lambda v: v.tensor_tensor(gcol[:, 4*G4:5*G4], gcol[:, 4*G4:5*G4], gcol[:, 5*G4:6*G4], ALU.subtract),
          ["gcolT"], ["gcolT"])
        E("vector", lambda v: v.tensor_copy(gstage[:, 0:8:2], gcol[:, 2*G4:3*G4]),
          ["gcolT"], ["gstage"])
        E("vector", lambda v: v.tensor_copy(gstage[:, 1:8:2], gcol[:, 4*G4:5*G4]),
          ["gcolT"], ["gstage"])
        E("scalar", lambda s: s.activation(out=gstage2[:], in_=gstage[:], func=AF.Identity),
          ["gstage"], ["gstage2"])
        E("gpsimd", lambda g2, k=k: g2.dma_start(
              out=cc_in[k].rearrange("a (p q) -> (a p) q", p=128), in_=gstage2[:]),
          ["gstage2"], ["ccin%d" % k], dma=True)
        E("gpsimd", lambda g2, k=k: g2.collective_compute(
              "AllGather", ALU.bypass, replica_groups=[list(range(NCORES))],
              ins=[cc_in[k].opt()], outs=[cc_out[k].opt()]),
          ["ccin%d" % k], ["ccout%d" % k], cc=True)
        E("gpsimd", lambda v: v.tensor_copy(gath[:, 0:8:2], pc(LR)), ["pcol"], ["gath"])
        E("gpsimd", lambda v: v.tensor_copy(gath[:, 1:8:2], pc(LI)), ["pcol"], ["gath"])
        E("gpsimd", lambda g2, k=k: g2.dma_start(
              out=gath[:, 8:72].rearrange("p (j q) -> p j q", q=8),
              in_=cc_out[k].rearrange("j (p q) -> p j q", p=128)),
          ["ccout%d" % k], ["gath"], dma=True)

        rho_b = pc(RH).rearrange("p (g o) -> p g o", o=1)
        for g in range(G4):
            ug, vg = u2[g], v2[g]
            ru, rv = UVR[g]
            rb1 = rho_b[:, g, :].broadcast_to((128, 1024))
            E("vector", lambda v, ug=ug, rb1=rb1: v.tensor_tensor_scan(
                  out=ug, data0=rb1, data1=ug, initial=0.0, op0=ALU.mult, op1=ALU.add),
              [ru, "pcol"], [ru])
            E("vector", lambda v, vg=vg, rb1=rb1: v.tensor_tensor_scan(
                  out=vg, data0=rb1, data1=vg, initial=0.0, op0=ALU.mult, op1=ALU.add),
              [rv, "pcol"], [rv])

        if "uv0" in taps and k == 0:
            E("sync", lambda s: s.dma_start(out=tap_d["uv0"][:, 0:1024], in_=u2[0]),
              [SCN[0]], dma=True)
            E("sync", lambda s: s.dma_start(out=tap_d["uv0"][:, 1024:2048], in_=v2[0]),
              [SCN[1]], dma=True)
        dre = dpre[:].rearrange("p (g i) -> p g i", i=16)
        dim_ = dpim[:].rearrange("p (g i) -> p g i", i=16)
        w9re_v = w9re[:].rearrange("p (j g) -> p j g", g=G4)
        w9im_v = w9im[:].rearrange("p (j g) -> p j g", g=G4)
        for c in range(NCORES):
            vr = dre[:, :, 8+c::-1][:, :, 0:9].transpose([0, 2, 1])
            vi = dim_[:, :, 8+c::-1][:, :, 0:9].transpose([0, 2, 1])
            if c == 0:
                E("vector", lambda v, vr=vr, c=c: v.tensor_scalar(
                      w9re_v, vr, scsel[:, c:c+1], None, ALU.mult),
                  ["dpre", "scsel"], ["w9re"])
                E("vector", lambda v, vi=vi, c=c: v.tensor_scalar(
                      w9im_v, vi, scsel[:, c:c+1], None, ALU.mult),
                  ["dpim", "scsel"], ["w9im"])
            else:
                E("vector", lambda v, vr=vr, c=c: v.scalar_tensor_tensor(
                      w9re_v, vr, scsel[:, c:c+1], w9re_v, ALU.mult, ALU.add),
                  ["dpre", "scsel", "w9re"], ["w9re"])
                E("vector", lambda v, vi=vi, c=c: v.scalar_tensor_tensor(
                      w9im_v, vi, scsel[:, c:c+1], w9im_v, ALU.mult, ALU.add),
                  ["dpim", "scsel", "w9im"], ["w9im"])
        gview = gath[:].rearrange("p (j g r) -> p j g r", g=G4, r=2)
        gre_v = gview[:, :, :, 0]
        gim_v = gview[:, :, :, 1]
        t9a_w = t9a[:].rearrange("p (g j) -> p j g", g=G4)
        t9b_w = t9b[:].rearrange("p (g j) -> p j g", g=G4)
        t9a_r = t9a[:].rearrange("p (g j) -> p g j", j=9)
        t9b_r = t9b[:].rearrange("p (g j) -> p g j", j=9)
        E("vector", lambda v: v.tensor_tensor(t9a_w, w9re_v, gre_v, ALU.mult),
          ["w9re", "gath"], ["t9a"])
        E("gpsimd", lambda v: v.tensor_tensor(t9b_w, w9im_v, gim_v, ALU.mult),
          ["w9im", "gath"], ["t9b"])
        E("vector", lambda v: v.tensor_tensor(t9a_r, t9a_r, t9b_r, ALU.subtract),
          ["t9a", "t9b"], ["t9a"])
        E("vector", lambda v: v.reduce_sum(gcol[:, 0:G4], t9a_r, axis=mybir.AxisListType.X),
          ["t9a"], ["gcolG"])
        E("vector", lambda v: v.tensor_tensor(t9a_w, w9re_v, gim_v, ALU.mult),
          ["w9re", "gath"], ["t9a"])
        E("gpsimd", lambda v: v.tensor_tensor(t9b_w, w9im_v, gre_v, ALU.mult),
          ["w9im", "gath"], ["t9b"])
        E("vector", lambda v: v.tensor_tensor(t9a_r, t9a_r, t9b_r, ALU.add),
          ["t9a", "t9b"], ["t9a"])
        E("vector", lambda v: v.reduce_sum(gcol[:, G4:2*G4], t9a_r, axis=mybir.AxisListType.X),
          ["t9a"], ["gcolG"])
        E("vector", lambda v: v.tensor_tensor(gcol[:, 2*G4:3*G4], pc(CO), gcol[:, 0:G4], ALU.mult),
          ["pcol", "gcolG"], ["gcolT"])
        E("vector", lambda v: v.tensor_tensor(gcol[:, 3*G4:4*G4], pc(SI), gcol[:, G4:2*G4], ALU.mult),
          ["pcol", "gcolG"], ["gcolT"])
        E("vector", lambda v: v.tensor_tensor(gcol[:, 2*G4:3*G4], gcol[:, 2*G4:3*G4], gcol[:, 3*G4:4*G4], ALU.subtract),
          ["gcolT"], ["gcolT"])
        E("vector", lambda v: v.tensor_tensor(gcol[:, 4*G4:5*G4], pc(SI), gcol[:, 0:G4], ALU.mult),
          ["pcol", "gcolG"], ["gcolT"])
        E("vector", lambda v: v.tensor_tensor(gcol[:, 5*G4:6*G4], pc(CO), gcol[:, G4:2*G4], ALU.mult),
          ["pcol", "gcolG"], ["gcolT"])
        E("vector", lambda v: v.tensor_tensor(gcol[:, 4*G4:5*G4], gcol[:, 4*G4:5*G4], gcol[:, 5*G4:6*G4], ALU.add),
          ["gcolT"], ["gcolT"])
        E("vector", lambda v: v.tensor_tensor(gcol[:, 6*G4:7*G4], pc(RH), gcol[:, 2*G4:3*G4], ALU.mult),
          ["pcol", "gcolT"], ["gcolC"])
        E("vector", lambda v: v.scalar_tensor_tensor(gcol[:, 7*G4:8*G4], pc(RH), -1.0, gcol[:, 4*G4:5*G4], ALU.mult, ALU.mult),
          ["pcol", "gcolT"], ["gcolC"])

        if "fin" in taps and k == 0:
            E("sync", lambda s: s.dma_start(out=tap_d["fin"][:, 0:16], in_=fcol[:]),
              ["fcol"], dma=True)
            E("sync", lambda s: s.dma_start(out=tap_d["fin"][:, 16:48], in_=gcol[:]),
              ["gcolC", "gcolG", "gcolT"], dma=True)
            E("sync", lambda s: s.dma_start(out=tap_d["fin"][:, 48:120], in_=gath[:]),
              ["gath"], dma=True)
        for g in range(G4):
            ug, vg = u2[g], v2[g]
            ru, rv = UVR[g]
            cv = cv16[g]
            rcv = "xq%d" % g
            E("vector", lambda v, g=g, ug=ug: v.scalar_tensor_tensor(
                  ug, p3[:, g, :], gcol[:, 6*G4+g:6*G4+g+1], ug, ALU.mult, ALU.add),
              [ru, "p0tab", "gcolC"], [ru])
            E("vector", lambda v, g=g, vg=vg: v.scalar_tensor_tensor(
                  vg, p3[:, g, :], gcol[:, 7*G4+g:7*G4+g+1], vg, ALU.mult, ALU.add),
              [rv, "p0tab", "gcolC"], [rv])
            E("vector", lambda v, g=g, ug=ug: v.tensor_tensor(ug, ug, c3[:, g, :], ALU.mult),
              [ru, "ctabA" if g < 3 else "ctabB"], [ru])
            E("gpsimd", lambda v, g=g, vg=vg: v.tensor_tensor(vg, vg, s3[:, g, :], ALU.mult),
              [rv, "stabA" if g < 3 else "stabB"], [rv])
            E("vector", lambda v, ug=ug, vg=vg: v.tensor_tensor(ug, ug, vg, ALU.add),
              [ru, rv], [ru])
            E("scalar", lambda s, g=g, ug=ug, cv=cv: s.activation(
                  out=cv, in_=ug, func=AF.Identity, scale=pc(SC)[:, g:g+1]),
              [ru, "pcol"], [rcv])
        for g in range(G4):
            cv = cv16[g]
            E("sync", lambda s, g=g, cv=cv: s.dma_start(
                  out=convT[:].rearrange("p (jj d) -> p jj d", d=DIM)[:, :, g*128:(g+1)*128],
                  in_=cv, transpose=True),
              ["xq%d" % g], ["cTbuf"], dma=True)

        if "convtap" in taps and k == 0:
            E("gpsimd", lambda s: s.dma_start(out=tap_d["convtap"][:, :], in_=convT[:]),
              ["cTbuf"], dma=True)

        TORD = [4, 5, 6, 7, 0, 1, 2, 3]
        for j in TORD:
            E("gpsimd", lambda v, j=j: v.tensor_tensor(
                  h[j][:], h[j][:], convT[:, j*DIM:(j+1)*DIM], ALU.add),
              ["h%d" % j, "cTbuf"], ["h%d" % j])
        for j in TORD:
            ln_stats(h[j][:], ["h%d" % j], j, 1)
        finalize_batch(4, 4, "vector")
        finalize_batch(0, 4, "vector")
        for j in TORD:
            q = j % 4
            z16 = xbuf16[:, q*1024:q*1024 + 512]
            rq = "xq%d" % q
            tmpx = tmp512a if j % 2 == 0 else tmp512b
            rt = "tmp512a" if j % 2 == 0 else "tmp512b"
            tmp16 = tmpx[:].bitcast(BF16)[:, 0:DIM]
            E("scalar", lambda s, j=j, tmp16=tmp16: s.activation(
                  out=tmp16, in_=h[j][:], func=AF.Identity,
                  scale=rstd8[:, j:j+1], bias=nmr8[:, j:j+1]),
              ["h%d" % j, "rstd8", "nmr8"], [rt])
            E("vector", lambda v, tmp16=tmp16: v.tensor_tensor(tmp16, tmp16, l2wb16[:], ALU.mult),
              [rt, "l2wb"], [rt])
            E("gpsimd", lambda v, tmp16=tmp16: v.tensor_tensor(tmp16, tmp16, l2bb16[:], ALU.add),
              [rt, "l2bb"], [rt])
            E("scalar", lambda s, z16=z16, tmp16=tmp16: s.activation(out=z16, in_=tmp16, func=AF.Identity),
              [rt], [rq])
            E("sync", lambda s, j=j, z16=z16: s.dma_start(
                  out=yT[:].rearrange("p (g t) -> p g t", t=1024)[:, :, j*128:(j+1)*128],
                  in_=z16, transpose=True),
              [rq], ["yTa" if j < 4 else "yTb"], dma=True)

        if k + 1 < DEPTH:
            load_ln_vectors(k + 1)

        for idx, st_ in enumerate((1, 0)):
            if idx == 0 and k + 1 < DEPTH:
                load_phz(k + 1)
                gen_tables_a()
            if idx == 1 and k + 1 < DEPTH:
                gen_tables_b()
            yreg = "yTb" if st_ == 1 else "yTa"
            for half in range(2):
                for i in range(8):
                    fi = half * 8 + i
                    for g in range(G4):
                        E("tensor", lambda t, fi=fi, g=g, st_=st_, i=i: t.matmul(
                              ps[i][:],
                              w1T16[:, g*DFF + fi*128: g*DFF + (fi+1)*128],
                              yT[:, g*1024 + st_*512: g*1024 + (st_+1)*512],
                              start=(g == 0), stop=(g == G4 - 1)),
                          ["wbufA", yreg], ["ps%d" % i], no_selfdep=True)
                for i in range(8):
                    fi = half * 8 + i
                    E("scalar", lambda s, fi=fi, i=i: s.activation(
                          out=z1T[:, fi*512:(fi+1)*512], in_=ps[i][:], func=AF.Silu,
                          bias=b1c[:, fi:fi+1]),
                      ["ps%d" % i, "b1c"], [Z1R[fi // 4]])
            for jj in range(4):
                j = st_ * 4 + jj
                pj = ps[jj % 2]
                rp = "ps%d" % (jj % 2)
                for g in range(FG):
                    E("tensor", lambda t, g=g, jj=jj, pj=pj: t.matmul(
                          pj[:],
                          z1T[:, g*512 + jj*128: g*512 + (jj+1)*128],
                          w2T16[:, g*DIM:(g+1)*DIM],
                          start=(g == 0), stop=(g == FG - 1)),
                      [Z1R[g // 4], "wbufB"], [rp], no_selfdep=True)
                E("scalar", lambda s, pj=pj: s.activation(
                      out=tmp512c[:], in_=pj[:], func=AF.Identity),
                  [rp], ["tmp512c"])
                E("vector", lambda v, j=j: v.tensor_tensor(h[j][:], h[j][:], tmp512c[:], ALU.add),
                  ["h%d" % j, "tmp512c"], ["h%d" % j])
                E("vector", lambda v, j=j: v.tensor_tensor(h[j][:], h[j][:], b2b[:], ALU.add),
                  ["h%d" % j, "b2b"], ["h%d" % j])
            if idx == 0 and k + 1 < DEPTH:
                emit_front([4, 5, 6, 7])
        if k + 1 < DEPTH:
            emit_front([0, 1, 2, 3])

        if ("h%d" % (k + 1)) in taps:
            for j in range(NT):
                E("sync", lambda s, j=j, k=k: s.dma_start(
                      out=tap_d["h%d" % (k+1)][j*128:(j+1)*128, :], in_=h[j][:]),
                  ["h%d" % j], dma=True)

        if k + 1 < DEPTH:
            load_w1(k + 1)
            load_w2(k + 1)
            load_b_vectors(k + 1)

    E("gpsimd", lambda g: g.dma_start(
          out=woutT16[:].rearrange("p (g v) -> p g v", v=D),
          in_=woutT_d.rearrange("(g p) v -> p g v", p=128)), writes=WBUF, dma=True)
    E("gpsimd", lambda g: g.dma_start(
          out=awrA16[:].rearrange("p (g n) -> p g n", n=128),
          in_=awrA_d.rearrange("(g p) n -> p g n", p=128)), writes=["p0tab"], dma=True)
    E("gpsimd", lambda g: g.dma_start(
          out=awrW16[:].rearrange("p (g n) -> p g n", n=H),
          in_=awrW_d.rearrange("(g p) n -> p g n", p=128)), writes=["awrW16"], dma=True)
    E("gpsimd", lambda g: g.dma_start(out=b16c[:], in_=bflat_d[:]),
      writes=["p0tab"], dma=True)
    E("gpsimd", lambda g: g.dma_start(out=boutb[:], in_=bout_d.partition_broadcast(128)),
      writes=SCN, dma=True)
    E("gpsimd", lambda g: g.dma_start(out=l1wb16[:], in_=lnlw_d.partition_broadcast(128)),
      writes=["l1wb"], dma=True)
    E("gpsimd", lambda g: g.dma_start(out=l1bb16[:], in_=lnlb_d.partition_broadcast(128)),
      writes=["l1bb"], dma=True)


    em.full_barrier()
    for j in range(NT):
        ln_stats(h[j][:], ["h%d" % j], j, 1)
    finalize_batch(0, NT, "vector")
    if "lnldbg" in taps:
        E("sync", lambda s: s.dma_start(out=tap_d["lnldbg"][:, 0:512], in_=h[0][:]),
          ["h0"], dma=True)
        E("vector", lambda v: v.tensor_copy(tmp512b[:, 0:8], rstd8[:]), ["rstd8"], ["tmp512b"])
        E("vector", lambda v: v.tensor_copy(tmp512b[:, 8:16], nmr8[:]), ["nmr8"], ["tmp512b"])
        E("vector", lambda v: v.tensor_copy(tmp512b[:, 16:32], mvst[:]), ["mvst"], ["tmp512b"])
        E("sync", lambda s: s.dma_start(out=tap_d["lnldbg"][:, 512:544], in_=tmp512b[:, 0:32]),
          ["tmp512b"], dma=True)
    for j in range(NT):
        q = j % 4
        hf16 = xbuf16[:, q*1024:q*1024 + 512]
        rq = "xq%d" % q
        tmpx = tmp512a if j % 2 == 0 else tmp512b
        rt = "tmp512a" if j % 2 == 0 else "tmp512b"
        tmp16 = tmpx[:].bitcast(BF16)[:, 0:DIM]
        E("scalar", lambda s, j=j, tmp16=tmp16: s.activation(
              out=tmp16, in_=h[j][:], func=AF.Identity,
              scale=rstd8[:, j:j+1], bias=nmr8[:, j:j+1]),
          ["h%d" % j, "rstd8", "nmr8"], [rt])
        E("vector", lambda v, tmp16=tmp16: v.tensor_tensor(tmp16, tmp16, l1wb16[:], ALU.mult),
          [rt, "l1wb"], [rt])
        E("gpsimd", lambda v, tmp16=tmp16: v.tensor_tensor(tmp16, tmp16, l1bb16[:], ALU.add),
          [rt, "l1bb"], [rt])
        E("scalar", lambda s, hf16=hf16, tmp16=tmp16: s.activation(out=hf16, in_=tmp16, func=AF.Identity),
          [rt], [rq])
        E("sync", lambda s, j=j, hf16=hf16: s.dma_start(
              out=hfT[:].rearrange("p (g t) -> p g t", t=1024)[:, :, j*128:(j+1)*128],
              in_=hf16, transpose=True),
          [rq], ["yTa" if j < 4 else "yTb"], dma=True)
        if "hf" in taps:
            E("vector", lambda v, hf16=hf16: v.tensor_copy(tmp512a[:], hf16),
              ["xq%d" % q], ["tmp512a"])
            E("sync", lambda s, j=j: s.dma_start(
                  out=tap_d["hf"][j*128:(j+1)*128, :], in_=tmp512a[:]),
              ["tmp512a"], dma=True)

    BIGR = [["ctabA", "ctabB"], ["stabA", "stabB"]]
    for b in range(4):
        tiles = [2*b, 2*b + 1]
        for ji, j in enumerate(tiles):
            bigj = big[ji]
            rbig = BIGR[ji]
            for vc in range(8):
                pv = ps[vc]
                rp = "ps%d" % vc
                for g in range(G4):
                    E("tensor", lambda t, g=g, j=j, vc=vc, pv=pv: t.matmul(
                          pv[:],
                          hfT[:, g*1024 + j*128: g*1024 + (j+1)*128],
                          woutT16[:, g*D + vc*512: g*D + (vc+1)*512],
                          start=(g == 0), stop=(g == G4 - 1)),
                      ["yTa", "yTb"] + WBUF, [rp], no_selfdep=True)
                E("vector", lambda v, vc=vc, pv=pv, bigj=bigj: v.scalar_tensor_tensor(
                      bigj[:, vc*512:(vc+1)*512], pv[:], 1.0, boutb[:, vc*512:(vc+1)*512],
                      ALU.mult, ALU.add, accum_out=statc[:, vc:vc+1]),
                  [rp] + SCN, rbig + ["statc"])
            E("scalar", lambda s, bigj=bigj: s.activation(
                  out=xbuf16[:], in_=bigj[:], func=AF.Square,
                  accum_out=stat2[:, 0:1]),
              rbig, XQ + ["stat2"])
            if "attn" in taps:
                E("sync", lambda s, j=j, bigj=bigj: s.dma_start(
                      out=tap_d["attn"][j*128:(j+1)*128, :], in_=bigj[:]),
                  rbig, dma=True)
            stats_from_sums(j, D)
        finalize_batch(2*b, 2, "vector")
        for ji, j in enumerate(tiles):
            bigj = big[ji]
            rbig = BIGR[ji]
            xs = xstage[ji]
            xsregs = XS0 if ji == 0 else ["xst1"]
            E("sync", lambda s, j=j, xs=xs: s.dma_start(out=xs[:], in_=x_d[j*128:(j+1)*128, :]),
              writes=xsregs, dma=True)
            E("scalar", lambda s, j=j, xs=xs: s.activation(
                  out=xs[:], in_=xs[:], func=AF.Identity, bias=nmr8[:, j:j+1]),
              xsregs + ["nmr8"], xsregs)
            E("vector", lambda v, j=j, bigj=bigj, xs=xs: v.scalar_tensor_tensor(
                  bigj[:], bigj[:], rstd8[:, j:j+1], xs[:], ALU.mult, ALU.add,
                  accum_out=statc[:, 0:1]),
              rbig + xsregs + ["rstd8"], rbig + ["statc"])
            E("scalar", lambda s, bigj=bigj: s.activation(
                  out=xbuf16[:], in_=bigj[:], func=AF.Square,
                  accum_out=stat2[:, 0:1]),
              rbig, XQ + ["stat2"])
            if "x1" in taps:
                E("sync", lambda s, j=j, bigj=bigj: s.dma_start(
                      out=tap_d["x1"][j*128:(j+1)*128, :], in_=bigj[:]),
                  rbig, dma=True)
            stats_from_sums(j, D, ncol=1)
        finalize_batch(2*b, 2, "vector")
        for ji, j in enumerate(tiles):
            bigj = big[ji]
            rbig = BIGR[ji]
            E("scalar", lambda s, j=j, bigj=bigj: s.activation(
                  out=xbuf16[:], in_=bigj[:], func=AF.Identity,
                  scale=rstd8[:, j:j+1], bias=nmr8[:, j:j+1]),
              rbig + ["rstd8", "nmr8"], XQ)
            E("sync", lambda s: s.dma_start(
                  out=x1lnT[:].rearrange("p (g t) -> p g t", g=DG), in_=xbuf16[:],
                  transpose=True),
              XQ, ["cTbuf"], dma=True)
            for g in range(DG):
                E("tensor", lambda t, g=g: t.matmul(
                      ps[4][:, 0:128], x1lnT[:, g*128:(g+1)*128],
                      awrA16[:, g*128:(g+1)*128],
                      start=(g == 0), stop=(g == DG - 1)),
                  ["cTbuf", "p0tab"], ["ps4"], no_selfdep=True)
            for g in range(DG):
                E("tensor", lambda t, g=g: t.matmul(
                      ps[5][:, 0:H], x1lnT[:, g*128:(g+1)*128],
                      awrW16[:, g*H:(g+1)*H],
                      start=(g == 0), stop=(g == DG - 1)),
                  ["cTbuf", "awrW16"], ["ps5"], no_selfdep=True)
            E("vector", lambda v: v.scalar_tensor_tensor(
                  logitsb, ps[5][:, 0:H], 1.0, brb[:], ALU.mult, ALU.add),
              ["ps5", "brb"], ["logitsb"])
            E("vector", lambda v: v.reduce_max(sinvc, logitsb, axis=mybir.AxisListType.X),
              ["logitsb"], ["sinvc"])
            E("vector", lambda v: v.tensor_scalar(sinvc, sinvc, -1.0, None, ALU.mult),
              ["sinvc"], ["sinvc"])
            E("scalar", lambda s: s.activation(out=exps, in_=logitsb, func=AF.Exp,
                                               bias=sinvc),
              ["logitsb", "sinvc"], ["exps"])
            E("vector", lambda v: v.reduce_sum(sinvc, exps, axis=mybir.AxisListType.X),
              ["exps"], ["sinvc"])
            E("vector", lambda v: v.reciprocal(sinvc, sinvc), ["sinvc"], ["sinvc"])
            E("vector", lambda v: v.tensor_scalar(sinvc, sinvc, 16.0, None, ALU.mult),
              ["sinvc"], ["sinvc"])
            eb = exps.unsqueeze(2).broadcast_to((128, H, 16))
            mlf = deadbuf[:, 384:512]
            E("vector", lambda v, eb=eb: v.tensor_tensor(
                  mlf.rearrange("p (hh r) -> p hh r", r=16),
                  ps[4][:, 0:128].rearrange("p (hh r) -> p hh r", r=16), eb, ALU.mult),
              ["ps4", "exps"], ["ml16"])
            E("tensor", lambda t: t.transpose(ps[6][:, 0:128], mlf, eyef[:]),
              ["ml16", "eyef"], ["ps6"])
            E("vector", lambda v: v.tensor_copy(mlT16, ps[6][:, 0:128]),
              ["ps6"], ["mlT16"])
            if "ffdbg" in taps and b == 0 and ji == 0:
                E("vector", lambda v: v.tensor_copy(tmp512a[:, 0:128], ps[4][:, 0:128]),
                  ["ps4"], ["tmp512a"])
                E("vector", lambda v: v.tensor_copy(tmp512a[:, 128:256], deadbuf[:, 384:512]),
                  ["ml16"], ["tmp512a"])
                E("vector", lambda v: v.tensor_copy(tmp512a[:, 256:384], mlT16),
                  ["mlT16"], ["tmp512a"])
                E("vector", lambda v: v.tensor_copy(tmp512a[:, 384:416], deadbuf[:, 0:32]),
                  ["logitsb", "exps", "sinvc"], ["tmp512a"])
                E("sync", lambda s: s.dma_start(out=tap_d["ffdbg"][:, 0:448],
                                                in_=tmp512a[:, 0:448]),
                  ["tmp512a"], dma=True)
            xsregs = XS0 if ji == 0 else ["xst1"]
            for oc in range(8):
                pv = ps[oc % 2]
                rp = "ps%d" % (oc % 2)
                E("tensor", lambda t, oc=oc, pv=pv: t.matmul(
                      pv[:], mlT16, b16c[:, oc*512:(oc+1)*512], start=True, stop=True),
                  ["mlT16", "p0tab"], [rp])
                E("scalar", lambda s, oc=oc, pv=pv, ji=ji: s.activation(
                      out=xstage[ji][:, oc*512:(oc+1)*512], in_=pv[:], func=AF.Identity,
                      scale=sinvc, accum_out=statc[:, oc:oc+1]),
                  [rp, "sinvc"], xsregs + ["statc"])
            if "ffn" in taps:
                E("sync", lambda s, j=j, ji=ji: s.dma_start(
                      out=tap_d["ffn"][j*128:(j+1)*128, :], in_=xstage[ji][:]),
                  xsregs, dma=True)
            E("scalar", lambda s, ji=ji: s.activation(
                  out=xbuf16[:], in_=xstage[ji][:], func=AF.Square,
                  accum_out=stat2[:, 0:1]),
              xsregs, XQ + ["stat2"])
            stats_from_sums(j, D)
        finalize_batch(2*b, 2, "vector")
        for ji, j in enumerate(tiles):
            bigj = big[ji]
            rbig = BIGR[ji]
            xsregs = XS0 if ji == 0 else ["xst1"]
            E("scalar", lambda s, j=j, ji=ji: s.activation(
                  out=xstage[ji][:], in_=xstage[ji][:], func=AF.Identity,
                  scale=rstd8[:, j:j+1], bias=nmr8[:, j:j+1]),
              xsregs + ["rstd8", "nmr8"], xsregs)
            E("gpsimd", lambda v, ji=ji, bigj=bigj: v.tensor_tensor(
                  xstage[ji][:], bigj[:], xstage[ji][:], ALU.add),
              rbig + xsregs, xsregs)
            E("sync", lambda s, j=j, ji=ji: s.dma_start(
                  out=out_d[j*128:(j+1)*128, :], in_=xstage[ji][:]),
              xsregs, dma=True)

    em.barrier_wait("vector", em.milestone("sdma_sync"))
    em.barrier_wait("vector", em.milestone("sdma_gp"))
    em.barrier_wait("vector", em.milestone("sdma_act"))

    sems = {s: es.enter_context(nc.semaphore(s)) for s in SEMS}
    with es:
        block = nc.Block()
        with block:
            em.replay(block, sems)
    return nc, tap_d



def _prep_inputs(inputs):
    g = {}
    x = np.ascontiguousarray(np.asarray(inputs["L0_attn_x"], np.float32).reshape(8192, D))
    Win = np.asarray(inputs["L0_attn_Win"], np.float32)
    g["winT"] = np.ascontiguousarray(Win.T)
    g["bin"] = np.asarray(inputs["L0_attn_bin"], np.float32).reshape(1, DIM)
    W1 = np.asarray(inputs["L0_attn_W1"], np.float32)
    g["w1T"] = np.ascontiguousarray(W1.transpose(0, 2, 1))
    W2 = np.asarray(inputs["L0_attn_W2"], np.float32)
    g["w2T"] = np.ascontiguousarray(W2.transpose(0, 2, 1))
    b1 = np.asarray(inputs["L0_attn_b1"], np.float32)
    g["b1c"] = np.ascontiguousarray(b1.reshape(DEPTH, FG, 128).transpose(0, 2, 1))
    g["b2"] = np.asarray(inputs["L0_attn_b2"], np.float32).reshape(DEPTH, 1, DIM)
    for nm, key in (("ln1w", "L0_attn_ln1_w"), ("ln1b", "L0_attn_ln1_b"),
                    ("ln2w", "L0_attn_ln2_w"), ("ln2b", "L0_attn_ln2_b")):
        g[nm] = np.asarray(inputs[key], np.float32).reshape(DEPTH, 1, DIM)
    for nm, key in (("phzre", "L0_attn_phz_re"), ("phzim", "L0_attn_phz_im"),
                    ("lcre", "L0_attn_lc_re"), ("lcim", "L0_attn_lc_im"),
                    ("sclc", "L0_attn_sc_lin")):
        v = np.asarray(inputs[key], np.float32)
        g[nm] = np.ascontiguousarray(v.reshape(DEPTH, G4, 128).transpose(0, 2, 1))
    g["lnlw"] = np.asarray(inputs["L0_attn_lnl_w"], np.float32).reshape(1, DIM)
    g["lnlb"] = np.asarray(inputs["L0_attn_lnl_b"], np.float32).reshape(1, DIM)
    Wout = np.asarray(inputs["L0_attn_Wout"], np.float32)
    g["woutT"] = np.ascontiguousarray(Wout.T)
    g["bout"] = np.asarray(inputs["L0_attn_bout"], np.float32).reshape(1, D)
    A = np.asarray(inputs["L0_ffn_A"], np.float32)
    Wr = np.asarray(inputs["L0_ffn_Wr"], np.float32)
    g["awrA"] = np.ascontiguousarray(A.reshape(128, D).T)
    g["awrW"] = np.ascontiguousarray(Wr.T)
    Bm = np.asarray(inputs["L0_ffn_B"], np.float32)
    g["bflat"] = np.ascontiguousarray(Bm.transpose(0, 2, 1).reshape(128, D))
    g["br"] = np.asarray(inputs["L0_ffn_br"], np.float32).reshape(1, H)
    g["eye"] = np.eye(128, dtype=np.float32)
    return x, g


def kernel(**inputs):
    x, g = _prep_inputs(inputs)
    nc, _ = build_program()
    in_maps = []
    for c in range(NCORES):
        m = dict(g)
        m["x"] = np.ascontiguousarray(x[c*T:(c+1)*T, :])
        m["csel"] = np.eye(NCORES, dtype=np.float32)[c:c+1, :]
        in_maps.append(m)
    res = run_bass_kernel_spmd(nc, in_maps, core_ids=list(range(NCORES)))
    out = np.concatenate([res.results[c]["out"] for c in range(NCORES)], axis=0)
    return out.reshape(1, 8192, D)


# version 40
# speedup vs baseline: 1.0598x; 1.0598x over previous
import numpy as np
import ml_dtypes
from contextlib import ExitStack

import concourse.bass as bass
import concourse.mybir as mybir
from concourse.bass_utils import run_bass_kernel_spmd

F32 = mybir.dt.float32
BF16 = mybir.dt.bfloat16
I32 = mybir.dt.int32
AF = mybir.ActivationFunctionType
ALU = mybir.AluOpType
BF = ml_dtypes.bfloat16
MAGIC = 0x5F3759DF

NCORES = 8
T = 1024
NT = 8
D = 4096
DG = 32
DIM = 512
G4 = 4
DFF = 2048
FG = 16
DEPTH = 4
H = 8
EPS = 1e-5

ENGINES = ("sync", "vector", "scalar", "gpsimd", "tensor")
SEMS = ("svec", "sact", "spool", "spe", "ssync", "sdma_sync", "sdma_gp", "sdma_act", "scc")


class Reg:
    __slots__ = ("name", "last_write", "readers")

    def __init__(self, name):
        self.name = name
        self.last_write = None
        self.readers = []


class Emitter:
    def __init__(self):
        self.ops = {e: [] for e in ENGINES}
        self.counts = {s: 0 for s in SEMS}
        self.watermark = {e: {} for e in ENGINES}

    def eng_sem(self, engine, dma):
        if dma:
            return {"sync": "sdma_sync", "gpsimd": "sdma_gp", "scalar": "sdma_act"}[engine]
        return {"vector": "svec", "scalar": "sact", "gpsimd": "spool",
                "tensor": "spe", "sync": "ssync"}[engine]

    def op(self, engine, fn, reads=(), writes=(), dma=False, cc=False, inc=None,
           no_selfdep=False):
        sem = "scc" if cc else self.eng_sem(engine, dma)
        if inc is None:
            inc = 16 if dma else 1
        needs = {}

        def need(dep):
            if dep is None:
                return
            s, c = dep
            if s == sem and no_selfdep:
                return
            if needs.get(s, 0) < c:
                needs[s] = c

        for r in reads:
            need(r.last_write)
        for w in writes:
            need(w.last_write)
            for rd in w.readers:
                need(rd)
        waits = []
        wm = self.watermark[engine]
        for s, c in sorted(needs.items()):
            if wm.get(s, 0) >= c:
                continue
            waits.append((s, c))
            wm[s] = c
        self.counts[sem] += inc
        cnt = self.counts[sem]
        self.ops[engine].append((waits, fn, sem, inc))
        for w in writes:
            w.last_write = (sem, cnt)
            w.readers = []
        for r in reads:
            r.readers.append((sem, cnt))
        return (sem, cnt)

    def barrier_wait(self, engine, dep):
        s, c = dep
        wm = self.watermark[engine]
        if wm.get(s, 0) >= c:
            return
        wm[s] = c
        self.ops[engine].append(([(s, c)], None, None, 0))

    def milestone(self, sem):
        return (sem, self.counts[sem])

    def full_barrier(self):
        ms = [(s, self.counts[s]) for s in SEMS if self.counts[s] > 0]
        for e in ENGINES:
            for m in ms:
                self.barrier_wait(e, m)

    def replay(self, block, sems):
        for ename in ENGINES:
            oplist = self.ops[ename]
            deco = getattr(block, ename)

            def make(oplist):
                def body(eng):
                    for waits, fn, sem, inc in oplist:
                        for s, c in waits:
                            eng.wait_ge(sems[s], c)
                        if fn is not None:
                            instr = fn(eng)
                            if sem is not None:
                                instr.then_inc(sems[sem], inc)
                return body
            deco(make(oplist))


def build_program(taps=()):
    taps = set(taps)
    nc = bass.Bass("TRN2", target_bir_lowering=False)
    es = ExitStack()

    def sb(name, shape, dtype):
        return es.enter_context(nc.sbuf_tensor(name, shape, dtype))

    def psum_t(name, shape, dtype=F32):
        return es.enter_context(nc.psum_tensor(name, shape, dtype))

    def dram_in(name, shape, dtype=F32):
        return nc.dram_tensor(name, shape, dtype, kind="ExternalInput").ap()

    x_d = dram_in("x", [T, D])
    winT_d = dram_in("winT", [D, DIM])
    bin_d = dram_in("bin", [1, DIM])
    w1T_d = dram_in("w1T", [DEPTH, DIM, DFF])
    w2T_d = dram_in("w2T", [DEPTH, DFF, DIM])
    b1_d = dram_in("b1c", [DEPTH, 128, FG])
    b2_d = dram_in("b2", [DEPTH, 1, DIM])
    lnv_d = {}
    for nm in ("ln1w", "ln1b", "ln2w", "ln2b"):
        lnv_d[nm] = dram_in(nm, [DEPTH, 1, DIM])
    phzre_d = dram_in("phzre", [DEPTH, 128, G4])
    phzim_d = dram_in("phzim", [DEPTH, 128, G4])
    lcre_d = dram_in("lcre", [DEPTH, 128, G4])
    lcim_d = dram_in("lcim", [DEPTH, 128, G4])
    scl_d = dram_in("sclc", [DEPTH, 128, G4])
    lnlw_d = dram_in("lnlw", [1, DIM])
    lnlb_d = dram_in("lnlb", [1, DIM])
    woutT_d = dram_in("woutT", [DIM, D])
    bout_d = dram_in("bout", [1, D])
    awrA_d = dram_in("awrA", [D, 128])
    awrW_d = dram_in("awrW", [D, H])
    bflat_d = dram_in("bflat", [128, D])
    br_d = dram_in("br", [1, H])
    eye_d = dram_in("eye", [128, 128])
    csel_d = dram_in("csel", [1, NCORES])

    out_d = nc.dram_tensor("out", [T, D], F32, kind="ExternalOutput").ap()

    tap_d = {}
    def deftap(name, shape):
        if name in taps:
            tap_d[name] = nc.dram_tensor("tap_" + name, shape, F32,
                                         kind="ExternalOutput").ap()
    deftap("h0", [T, DIM])
    deftap("tabs", [128, 12288])
    deftap("ytap", [128, D])
    deftap("uv0", [128, 2048])
    deftap("convtap", [128, 4096])
    deftap("fin", [128, 120])
    deftap("pcoltap", [128, 64])
    deftap("ffdbg", [128, 448])
    deftap("lnldbg", [128, 1568])
    for k in range(DEPTH):
        deftap("h%d" % (k + 1), [T, DIM])
    deftap("hf", [T, DIM])
    deftap("attn", [T, D])
    deftap("x1", [T, D])
    deftap("ffn", [T, D])

    cc_in = [nc.dram_tensor("cc_in%d" % k, [1, 1024], F32).ap() for k in range(DEPTH)]
    cc_out = [nc.dram_tensor("cc_out%d" % k, [NCORES, 1024], F32).ap() for k in range(DEPTH)]

    h = [sb("h%d" % j, [128, DIM], F32) for j in range(NT)]
    xstage = [sb("xst%d" % i, [128, D], F32) for i in range(2)]
    xbuf16 = sb("xbuf16", [128, D], BF16)
    xT16 = [sb("xT16_%d" % i, [128, D], BF16) for i in range(2)]
    wbuf = sb("wbuf", [128, 16384], BF16)
    tabbuf = sb("tabbuf", [128, 12288], F32)
    scanbuf = sb("scanbuf", [128, 4096], F32)
    deadbuf = sb("deadbuf", [128, 512], F32)
    z1T = sb("z1T", [128, 8192], BF16)
    awrW16 = sb("awrW16", [128, DG * H], BF16)

    ones16 = sb("ones16", [1, 128], BF16)
    binr16 = sb("binr16", [1, DIM], BF16)
    b2r16 = sb("b2r16", [1, DIM], BF16)
    l1wb16 = sb("l1wb16", [128, DIM], BF16)
    l1bb16 = sb("l1bb16", [128, DIM], BF16)
    l2wb16 = sb("l2wb16", [128, DIM], BF16)
    l2bb16 = sb("l2bb16", [128, DIM], BF16)
    b2b = sb("b2b", [128, DIM], F32)
    b1c = sb("b1cs", [128, FG], F32)
    brb = sb("brb", [128, H], F32)
    eyef = sb("eyef", [128, 128], F32)
    eye16 = sb("eye16", [128, 128], BF16)
    scsel = sb("scsel", [128, NCORES], F32)

    pcol = sb("pcol", [128, 16 * G4], F32)
    dpre = sb("dpre", [128, G4 * 16], F32)
    dpim = sb("dpim", [128, G4 * 16], F32)
    w9re = sb("w9re", [128, 9 * G4], F32)
    w9im = sb("w9im", [128, 9 * G4], F32)
    t9a = sb("t9a", [128, G4 * 9], F32)
    t9b = sb("t9b", [128, G4 * 9], F32)
    gath = sb("gath", [128, 72], F32)
    gcol = sb("gcol", [128, 8 * G4], F32)
    fcol = sb("fcol", [128, 16], F32)
    gstage = sb("gstage", [128, 8], F32)
    gstage2 = sb("gstage2", [128, 8], F32)

    mvst = sb("mvst", [128, 16], F32)
    rstd8 = sb("rstd8", [128, NT], F32)
    nmr8 = sb("nmr8", [128, NT], F32)
    bnbuf = sb("bnbuf", [128, 48], F32)
    nsc = [sb("nsc%d" % i, [128, 8], F32) for i in range(4)]
    nsci = sb("nsci", [128, 8], I32)
    statc = sb("statc", [128, 8], F32)
    stat2 = sb("stat2", [128, 8], F32)
    tmp512a = sb("tmp512a", [128, DIM], F32)
    tmp512b = sb("tmp512b", [128, DIM], F32)
    tmp512c = sb("tmp512c", [128, DIM], F32)

    ps = [psum_t("ps%d" % i, [128, 512], F32) for i in range(8)]

    winT16 = wbuf
    w1T16 = wbuf[:, 0:G4 * DFF]
    w2T16 = wbuf[:, G4 * DFF:]
    woutT16 = wbuf
    ctab = tabbuf[:, 0:4096]
    stab = tabbuf[:, 4096:8192]
    p0tab = tabbuf[:, 8192:12288]
    c3 = ctab.rearrange("p (g t) -> p g t", t=1024)
    s3 = stab.rearrange("p (g t) -> p g t", t=1024)
    p3 = p0tab.rearrange("p (g t) -> p g t", t=1024)
    yT = xT16[0]
    convT = xT16[1]
    z1Tf = z1T[:].bitcast(F32)
    u2 = [scanbuf[:, 0:1024], scanbuf[:, 2048:3072],
          z1Tf[:, 0:1024], z1Tf[:, 2048:3072]]
    v2 = [scanbuf[:, 1024:2048], scanbuf[:, 3072:4096],
          z1Tf[:, 1024:2048], z1Tf[:, 3072:4096]]
    cv16 = [xbuf16[:, 0:1024], xbuf16[:, 1024:2048],
            xbuf16[:, 2048:3072], xbuf16[:, 3072:4096]]
    tg1 = xstage[0][:, 0:2048]
    tg2 = xstage[0][:, 2048:4096]
    tg1v = tg1.rearrange("p (g t) -> p g t", t=512)
    tg2v = tg2.rearrange("p (g t) -> p g t", t=512)
    big = [tabbuf[:, 0:4096], tabbuf[:, 4096:8192]]
    b16c = tabbuf[:, 8192:10240].bitcast(BF16)
    awrA16 = tabbuf[:, 10240:12288].bitcast(BF16)
    boutb = scanbuf
    exps = deadbuf[:, 0:8]
    logitsb = deadbuf[:, 8:16]
    sinvc = deadbuf[:, 16:17]
    ml16 = deadbuf[:, 256:320].bitcast(BF16)
    mlT16 = deadbuf[:, 320:384].bitcast(BF16)
    hfT = yT
    x1lnT = convT

    em = Emitter()
    R = {}
    def reg(name):
        if name not in R:
            R[name] = Reg(name)
        return R[name]

    def E(engine, fn, reads=(), writes=(), **kw):
        em.op(engine, fn, reads=[reg(r) for r in reads],
              writes=[reg(w) for w in writes], **kw)

    WBUF = ["wbufA", "wbufB"]
    TAB = ["ctabA", "ctabB", "stabA", "stabB", "p0tab"]
    XQ = ["xq0", "xq1", "xq2", "xq3"]
    SCN = ["scn0", "scn1", "scn2", "scn3"]
    UVR = [("scn0", "scn1"), ("scn2", "scn3"), ("z1q0", "z1q1"), ("z1q2", "z1q3")]
    Z1R = ["z1q0", "z1q1", "z1q2", "z1q3"]
    XS0 = ["xst0a", "xst0b"]

    E("gpsimd", lambda g: g.dma_start(out=eyef[:], in_=eye_d[:]), writes=["eyef"], dma=True)
    E("gpsimd", lambda g: g.dma_start(out=scsel[:], in_=csel_d.partition_broadcast(128)),
      writes=["scsel"], dma=True)
    E("gpsimd", lambda g: g.dma_start(
        out=winT16[:].rearrange("p (g d) -> p g d", d=DIM),
        in_=winT_d.rearrange("(g p) d -> p g d", p=128)), writes=WBUF, dma=True)
    E("gpsimd", lambda g: g.dma_start(out=brb[:], in_=br_d.partition_broadcast(128)),
      writes=["brb"], dma=True)
    E("vector", lambda v: v.tensor_copy(eye16[:], eyef[:]), ["eyef"], ["eye16"])
    E("vector", lambda v: v.memset(ones16[:], 1.0), [], ["ones16"])
    E("vector", lambda v: v.memset(deadbuf[:, 17:18], float(0.25 * 2.0 * np.pi * (1.0 - 3e-7))),
      [], ["qbias"])
    E("vector", lambda v: v.memset(deadbuf[:, 18:19], EPS), [], ["epsc"])
    E("gpsimd", lambda g: g.dma_start(out=binr16[:], in_=bin_d[:]), writes=["binr16"], dma=True)

    def newton_rsqrt(eng, out_ap, out_w, var_ap, var_reads, width, add_eps):
        a = nsc[0][:, 0:width]
        if add_eps:
            E("scalar", lambda s: s.activation(out=a, in_=var_ap, func=AF.Sqrt,
                                               bias=deadbuf[:, 18:19]),
              var_reads + ["epsc"], ["nsc0"])
        else:
            E("scalar", lambda s: s.activation(out=a, in_=var_ap, func=AF.Sqrt),
              var_reads, ["nsc0"])
        E("vector", lambda v: v.reciprocal(out_ap, a), ["nsc0"], [out_w])

    def ln_stats(src_ap, src_regs, j, chunks):
        for cch in range(chunks):
            E("vector", lambda v, cch=cch: v.bn_stats(
                  bnbuf[:, cch*6:(cch+1)*6], src_ap[:, cch*512:(cch+1)*512]),
              src_regs, ["bnbuf"])
        E("vector", lambda v, j=j, cch=chunks: v.bn_aggr(mvst[:, 2*j:2*j+2],
                                                         bnbuf[:, 0:6*cch]),
          ["bnbuf"], ["mvst"])

    def stats_from_sums(j, nelem, ncol=8):
        if ncol > 1:
            E("vector", lambda v, j=j, nc=ncol: v.tensor_reduce(
                  statc[:, 0:1], statc[:, 0:nc], mybir.AxisListType.X, ALU.add),
              ["statc"], ["statc"])
        E("vector", lambda v, j=j, ne=nelem: v.tensor_scalar(mvst[:, 2*j:2*j+1], statc[:, 0:1],
                                                             1.0/ne, None, ALU.mult),
          ["statc"], ["mvst"])
        E("vector", lambda v, j=j: v.tensor_tensor(statc[:, 1:2], mvst[:, 2*j:2*j+1],
                                                   mvst[:, 2*j:2*j+1], ALU.mult),
          ["mvst"], ["statc"])
        E("vector", lambda v, j=j, ne=nelem: v.scalar_tensor_tensor(
              mvst[:, 2*j+1:2*j+2], stat2[:, 0:1], 1.0/ne, statc[:, 1:2],
              ALU.mult, ALU.subtract),
          ["stat2", "statc"], ["mvst"])

    def finalize_batch(j0, n, eng):
        var_ap = mvst[:, 2*j0+1: 2*(j0+n): 2]
        mu_ap = mvst[:, 2*j0: 2*(j0+n): 2]
        newton_rsqrt(eng, rstd8[:, j0:j0+n], "rstd8", var_ap, ["mvst"], n, True)
        E(eng, lambda g: g.tensor_tensor(nmr8[:, j0:j0+n], mu_ap,
                                         rstd8[:, j0:j0+n], ALU.mult),
          ["mvst", "rstd8"], ["nmr8"])
        E(eng, lambda g: g.tensor_scalar(nmr8[:, j0:j0+n], nmr8[:, j0:j0+n], -1.0,
                                         None, ALU.mult),
          ["nmr8"], ["nmr8"])

    pendingA = []
    def load_x(j):
        sl = j % 2
        xsj = xstage[sl]
        xr = XS0 if sl == 0 else ["xst1"]
        E("sync", lambda s, j=j, xsj=xsj: s.dma_start(out=xsj[:], in_=x_d[j*128:(j+1)*128, :]),
          writes=xr, dma=True)
    load_x(0)
    for j in range(NT):
        sl = j % 2
        xs = xstage[sl]
        xsregs = XS0 if sl == 0 else ["xst1"]
        if j + 1 < NT:
            load_x(j + 1)
        ln_stats(xs[:], xsregs, j, 8)
        finalize_batch(j, 1, "vector")
        E("scalar", lambda s, j=j, xs=xs: s.activation(
              out=xbuf16[:], in_=xs[:], func=AF.Identity,
              scale=rstd8[:, j:j+1], bias=nmr8[:, j:j+1]),
          xsregs + ["rstd8", "nmr8"], XQ)
        xt = xT16[sl]
        rxt = ["yTa", "yTb"] if sl == 0 else ["cTbuf"]
        E("sync", lambda s, xt=xt: s.dma_start(
              out=xt[:].rearrange("p (g t) -> p g t", g=DG), in_=xbuf16[:], transpose=True),
          XQ, rxt, dma=True)
        rp = "ps%d" % (j % 2)
        for g in range(DG):
            E("tensor", lambda t, g=g, xt=xt, pj=ps[j % 2]: t.matmul(
                  pj[:], xt[:, g*128:(g+1)*128], winT16[:, g*DIM:(g+1)*DIM],
                  start=(g == 0), stop=False),
              rxt + WBUF, [rp], no_selfdep=True)
        E("tensor", lambda t, pj=ps[j % 2]: t.matmul(
              pj[:], ones16[:], binr16[:], start=False, stop=True),
          ["ones16", "binr16"], [rp], no_selfdep=True)
        pendingA.append((j, rp))
        if len(pendingA) >= 2:
            jj2, rp2 = pendingA.pop(0)
            E("scalar", lambda s, jj2=jj2, pj=ps[jj2 % 2]: s.activation(
                  out=h[jj2][:], in_=pj[:], func=AF.Identity),
              [rp2], ["h%d" % jj2])
    while pendingA:
        jj2, rp2 = pendingA.pop(0)
        E("scalar", lambda s, jj2=jj2, pj=ps[jj2 % 2]: s.activation(
              out=h[jj2][:], in_=pj[:], func=AF.Identity),
          [rp2], ["h%d" % jj2])

    if "h0" in taps:
        for j in range(NT):
            E("sync", lambda s, j=j: s.dma_start(
                  out=tap_d["h0"][j*128:(j+1)*128, :], in_=h[j][:]),
              ["h%d" % j], dma=True)

    PR, PI, LR, LI, SC, CO, SI, RH, R2, S1, S2, S3 = range(12)
    def pc(i):
        return pcol[:, i*G4:(i+1)*G4]

    def load_phz(k):
        for slot, d in ((PR, phzre_d), (PI, phzim_d), (LR, lcre_d), (LI, lcim_d),
                        (SC, scl_d)):
            E("gpsimd", lambda g, slot=slot, d=d, k=k: g.dma_start(out=pc(slot), in_=d[k]),
              writes=["pcol"], dma=True)

    def load_ln_vectors(k):
        E("gpsimd", lambda g, k=k: g.dma_start(out=l1wb16[:], in_=lnv_d["ln1w"][k].partition_broadcast(128)), writes=["l1wb"], dma=True)
        E("gpsimd", lambda g, k=k: g.dma_start(out=l1bb16[:], in_=lnv_d["ln1b"][k].partition_broadcast(128)), writes=["l1bb"], dma=True)
        E("gpsimd", lambda g, k=k: g.dma_start(out=l2wb16[:], in_=lnv_d["ln2w"][k].partition_broadcast(128)), writes=["l2wb"], dma=True)
        E("gpsimd", lambda g, k=k: g.dma_start(out=l2bb16[:], in_=lnv_d["ln2b"][k].partition_broadcast(128)), writes=["l2bb"], dma=True)

    def load_b_vectors(k):
        E("gpsimd", lambda g, k=k: g.dma_start(out=b2b[:], in_=b2_d[k].partition_broadcast(128)), writes=["b2b"], dma=True)
        E("gpsimd", lambda g, k=k: g.dma_start(out=b1c[:], in_=b1_d[k]), writes=["b1c"], dma=True)

    def load_w1(k):
        E("gpsimd", lambda g, k=k: g.dma_start(
              out=w1T16.rearrange("p (g f) -> p g f", f=DFF),
              in_=w1T_d[k].rearrange("(g p) f -> p g f", p=128)),
          writes=["wbufA"], dma=True)

    def load_w2(k):
        E("gpsimd", lambda g, k=k: g.dma_start(
              out=w2T16.rearrange("p (g d) -> p g d", d=DIM),
              in_=w2T_d[k].rearrange("(g p) d -> p g d", p=128)),
          writes=["wbufB"], dma=True)

    TWOPI_S = float(2.0 * np.pi * (1.0 - 3e-7))
    INV2PI = float(1.0 / (2.0 * np.pi))

    def gen_tables_a():
        T2P, LNR, MSK, SGS = 12, 13, 14, 15
        E("vector", lambda v: v.tensor_tensor(pc(R2), pc(PR), pc(PR), ALU.mult),
          ["pcol"], ["pcol"])
        E("vector", lambda v: v.scalar_tensor_tensor(pc(S1), pc(PI), 1.0, pc(PI), ALU.mult, ALU.mult),
          ["pcol"], ["pcol"])
        E("vector", lambda v: v.tensor_tensor(pc(R2), pc(R2), pc(S1), ALU.add),
          ["pcol"], ["pcol"])
        E("vector", lambda v: v.tensor_scalar(pc(R2), pc(R2), 1.0, 1e-30, ALU.mult, ALU.add),
          ["pcol"], ["pcol"])
        newton_rsqrt("vector", pc(S1), "pcol", pc(R2), ["pcol"], G4, False)
        E("vector", lambda v: v.tensor_tensor(pc(S2), pc(R2), pc(S1), ALU.mult),
          ["pcol"], ["pcol"])
        E("vector", lambda v: v.tensor_tensor(pc(CO), pc(PR), pc(S1), ALU.mult),
          ["pcol"], ["pcol"])
        E("vector", lambda v: v.tensor_tensor(pc(SI), pc(PI), pc(S1), ALU.mult),
          ["pcol"], ["pcol"])
        E("scalar", lambda s: s.activation(out=pc(RH), in_=pc(S2), func=AF.Tanh),
          ["pcol"], ["pcol"])
        E("vector", lambda v: v.tensor_scalar(pc(R2), pc(CO), 1.0, 1.0, ALU.mult, ALU.add),
          ["pcol"], ["pcol"])
        E("vector", lambda v: v.tensor_scalar(pc(S1), pc(CO), -1.0, 1.0, ALU.mult, ALU.add),
          ["pcol"], ["pcol"])
        E("vector", lambda v: v.tensor_scalar(pc(R2), pc(R2), 1.0, None, ALU.max),
          ["pcol"], ["pcol"])
        E("vector", lambda v: v.tensor_scalar(pc(S1), pc(S1), 1.0, None, ALU.max),
          ["pcol"], ["pcol"])
        E("vector", lambda v: v.reciprocal(pcol[:, R2*G4:R2*G4+2*G4],
                                           pcol[:, R2*G4:R2*G4+2*G4]),
          ["pcol"], ["pcol"])
        E("vector", lambda v: v.tensor_scalar(pc(MSK), pc(CO), 0.0, None, ALU.is_ge),
          ["pcol"], ["pcol"])
        E("vector", lambda v: v.tensor_tensor(pc(S2), pc(SI), pc(R2), ALU.mult),
          ["pcol"], ["pcol"])
        E("vector", lambda v: v.tensor_tensor(pc(S3), pc(SI), pc(S1), ALU.mult),
          ["pcol"], ["pcol"])
        E("vector", lambda v: v.tensor_tensor(pc(S2), pc(S2), pc(S3), ALU.subtract),
          ["pcol"], ["pcol"])
        E("vector", lambda v: v.tensor_tensor(pc(S2), pc(S2), pc(MSK), ALU.mult),
          ["pcol"], ["pcol"])
        E("vector", lambda v: v.tensor_tensor(pc(S2), pc(S2), pc(S3), ALU.add),
          ["pcol"], ["pcol"])
        E("scalar", lambda s: s.activation(out=pc(S2), in_=pc(S2), func=AF.Arctan),
          ["pcol"], ["pcol"])
        E("vector", lambda v: v.tensor_scalar(pc(SGS), pc(SI), 0.0, None, ALU.is_lt),
          ["pcol"], ["pcol"])
        E("vector", lambda v: v.tensor_scalar(pc(R2), pc(MSK), 4.0, -2.0, ALU.mult, ALU.add),
          ["pcol"], ["pcol"])
        E("vector", lambda v: v.tensor_tensor(pc(R2), pc(R2), pc(S2), ALU.mult),
          ["pcol"], ["pcol"])
        E("vector", lambda v: v.tensor_scalar(pc(S1), pc(SGS), float(-2.0*np.pi), float(np.pi),
                                              ALU.mult, ALU.add),
          ["pcol"], ["pcol"])
        E("vector", lambda v: v.tensor_scalar(pc(S3), pc(MSK), -1.0, 1.0, ALU.mult, ALU.add),
          ["pcol"], ["pcol"])
        E("vector", lambda v: v.tensor_tensor(pc(S1), pc(S1), pc(S3), ALU.mult),
          ["pcol"], ["pcol"])
        E("vector", lambda v: v.tensor_tensor(pc(T2P), pc(R2), pc(S1), ALU.add),
          ["pcol"], ["pcol"])
        E("vector", lambda v: v.tensor_scalar(pc(T2P), pc(T2P), INV2PI, None, ALU.mult),
          ["pcol"], ["pcol"])
        E("vector", lambda v: v.tensor_scalar(pc(SGS), pc(RH), -1.0, 1.0, ALU.mult, ALU.add),
          ["pcol"], ["pcol"])
        E("vector", lambda v: v.tensor_scalar(pc(S1), pc(SGS), 0.25, 1.0/3.0, ALU.mult, ALU.add),
          ["pcol"], ["pcol"])
        E("vector", lambda v: v.tensor_tensor(pc(S1), pc(S1), pc(SGS), ALU.mult),
          ["pcol"], ["pcol"])
        E("vector", lambda v: v.tensor_scalar(pc(S1), pc(S1), 1.0, 0.5, ALU.mult, ALU.add),
          ["pcol"], ["pcol"])
        E("vector", lambda v: v.tensor_tensor(pc(S1), pc(S1), pc(SGS), ALU.mult),
          ["pcol"], ["pcol"])
        E("vector", lambda v: v.tensor_scalar(pc(S1), pc(S1), 1.0, 1.0, ALU.mult, ALU.add),
          ["pcol"], ["pcol"])
        E("vector", lambda v: v.tensor_tensor(pc(S1), pc(S1), pc(SGS), ALU.mult),
          ["pcol"], ["pcol"])
        E("vector", lambda v: v.tensor_scalar(pc(S1), pc(S1), -1.0, None, ALU.mult),
          ["pcol"], ["pcol"])
        E("scalar", lambda s: s.activation(out=pc(LNR), in_=pc(RH), func=AF.Ln),
          ["pcol"], ["pcol"])
        E("vector", lambda v: v.tensor_scalar(pc(MSK), pc(SGS), 0.1, None, ALU.is_lt),
          ["pcol"], ["pcol"])
        E("vector", lambda v: v.tensor_tensor(pc(S1), pc(S1), pc(LNR), ALU.subtract),
          ["pcol"], ["pcol"])
        E("vector", lambda v: v.tensor_tensor(pc(S1), pc(S1), pc(MSK), ALU.mult),
          ["pcol"], ["pcol"])
        E("vector", lambda v: v.tensor_tensor(pc(LNR), pc(LNR), pc(S1), ALU.add),
          ["pcol"], ["pcol"])
        iotq = scanbuf[:, 0:1024]
        onesq = scanbuf[:, 1024:2048]
        q4 = xstage[0][:].rearrange("p (g t) -> p g t", t=1024)
        qp4 = xstage[1][:].rearrange("p (g t) -> p g t", t=1024)
        t2pb = pc(T2P).rearrange("p (g o) -> p g o", o=1).broadcast_to((128, G4, 1024))
        lnrb = pc(LNR).rearrange("p (g o) -> p g o", o=1).broadcast_to((128, G4, 1024))
        iotb = iotq.unsqueeze(1).broadcast_to((128, G4, 1024))
        ci32 = ctab.bitcast(I32)
        TABC = ["ctabA", "ctabB"]
        TABS = ["stabA", "stabB"]
        E("vector", lambda v: v.memset(onesq, 1.0), [], ["scn1"])
        E("vector", lambda v: v.tensor_tensor_scan(
              out=iotq, data0=onesq, data1=onesq, initial=-1.0, op0=ALU.mult, op1=ALU.add),
          ["scn1"], ["scn0"])
        E("vector", lambda v: v.tensor_tensor(q4, iotb, t2pb, ALU.mult),
          ["scn0", "pcol"], XS0)
        E("vector", lambda v: v.tensor_tensor(qp4, iotb, lnrb, ALU.mult),
          ["scn0", "pcol"], ["xst1"])
        E("gpsimd", lambda g: g.tensor_copy(ci32[:], xstage[0][:]), XS0, TABC)
        E("gpsimd", lambda g: g.tensor_copy(stab[:], ci32[:]), TABC, TABS)
        E("gpsimd", lambda g: g.tensor_tensor(ctab[:], xstage[0][:], stab[:], ALU.subtract),
          TABS + XS0, TABC)
        E("vector", lambda v: v.tensor_scalar(xstage[0][:], ctab[:], 0.25, None, ALU.is_ge),
          TABC, XS0)
        E("gpsimd", lambda g: g.tensor_tensor(xstage[0][:], ctab[:], xstage[0][:], ALU.subtract),
          XS0 + TABC, XS0)

    def gen_tables_b():
        TABC = ["ctabA", "ctabB"]
        TABS = ["stabA", "stabB"]
        E("scalar", lambda s: s.activation(out=p0tab[:], in_=xstage[1][:], func=AF.Exp),
          ["xst1"], ["p0tab"])
        E("scalar", lambda s: s.activation(out=stab[:], in_=ctab[:], func=AF.Sin, scale=TWOPI_S),
          TABC, TABS)
        E("scalar", lambda s: s.activation(out=ctab[:], in_=xstage[0][:], func=AF.Sin,
                                           scale=TWOPI_S, bias=deadbuf[:, 17:18]),
          XS0 + ["qbias"], TABC)
        c1023 = c3[:, :, 1023]
        s1023 = s3[:, :, 1023]
        p1023 = p3[:, :, 1023]
        TABR = ["ctabA", "ctabB", "stabA", "stabB"]
        E("gpsimd", lambda v: v.tensor_tensor(pc(S2), p1023, pc(RH), ALU.mult),
          ["p0tab", "pcol"], ["pcol"])
        E("gpsimd", lambda v: v.tensor_tensor(pc(S1), c1023, pc(CO), ALU.mult), TABR + ["pcol"], ["pcol"])
        E("gpsimd", lambda v: v.tensor_tensor(pc(S3), s1023, pc(SI), ALU.mult), TABR + ["pcol"], ["pcol"])
        E("gpsimd", lambda v: v.tensor_tensor(pc(S1), pc(S1), pc(S3), ALU.subtract), ["pcol"], ["pcol"])
        E("gpsimd", lambda v: v.tensor_tensor(pc(S3), s1023, pc(CO), ALU.mult), TABR + ["pcol"], ["pcol"])
        E("gpsimd", lambda v: v.tensor_tensor(pc(R2), c1023, pc(SI), ALU.mult), TABR + ["pcol"], ["pcol"])
        E("gpsimd", lambda v: v.tensor_tensor(pc(S3), pc(S3), pc(R2), ALU.add), ["pcol"], ["pcol"])
        dre = dpre[:].rearrange("p (g i) -> p g i", i=16)
        dim_ = dpim[:].rearrange("p (g i) -> p g i", i=16)
        E("gpsimd", lambda v: v.memset(dpre[:], 0.0), [], ["dpre"])
        E("gpsimd", lambda v: v.memset(dpim[:], 0.0), [], ["dpim"])
        E("gpsimd", lambda v: v.memset(dre[:, :, 8:9], 1.0), ["dpre"], ["dpre"])
        E("gpsimd", lambda v: v.tensor_tensor(gcol[:, 0:G4], pc(S2), pc(S1), ALU.mult),
          ["pcol"], ["gcolD"])
        E("gpsimd", lambda v: v.tensor_tensor(gcol[:, G4:2*G4], pc(S2), pc(S3), ALU.mult),
          ["pcol"], ["gcolD"])
        for kk2 in range(1, 8):
            prc, pic = dre[:, :, 7+kk2], dim_[:, :, 7+kk2]
            E("gpsimd", lambda v, prc=prc: v.tensor_tensor(t9a[:, 0:G4], prc, gcol[:, 0:G4], ALU.mult),
              ["dpre", "gcolD"], ["t9a"])
            E("gpsimd", lambda v, pic=pic: v.tensor_tensor(t9b[:, 0:G4], pic, gcol[:, G4:2*G4], ALU.mult),
              ["dpim", "gcolD"], ["t9b"])
            E("gpsimd", lambda v, kk2=kk2: v.tensor_tensor(dre[:, :, 8+kk2], t9a[:, 0:G4], t9b[:, 0:G4], ALU.subtract),
              ["t9a", "t9b"], ["dpre"])
            E("gpsimd", lambda v, prc=prc: v.tensor_tensor(t9a[:, 0:G4], prc, gcol[:, G4:2*G4], ALU.mult),
              ["dpre", "gcolD"], ["t9a"])
            E("gpsimd", lambda v, pic=pic: v.tensor_tensor(t9b[:, 0:G4], pic, gcol[:, 0:G4], ALU.mult),
              ["dpim", "gcolD"], ["t9b"])
            E("gpsimd", lambda v, kk2=kk2: v.tensor_tensor(dim_[:, :, 8+kk2], t9a[:, 0:G4], t9b[:, 0:G4], ALU.add),
              ["t9a", "t9b"], ["dpim"])
        w9re_v = w9re[:].rearrange("p (j g) -> p j g", g=G4)
        w9im_v = w9im[:].rearrange("p (j g) -> p j g", g=G4)
        for c in range(NCORES):
            vr = dre[:, :, 8+c::-1][:, :, 0:9].transpose([0, 2, 1])
            vi = dim_[:, :, 8+c::-1][:, :, 0:9].transpose([0, 2, 1])
            if c == 0:
                E("vector", lambda v, vr=vr: v.tensor_scalar(
                      w9re_v, vr, scsel[:, 0:1], None, ALU.mult),
                  ["dpre", "scsel"], ["w9re"])
                E("vector", lambda v, vi=vi: v.tensor_scalar(
                      w9im_v, vi, scsel[:, 0:1], None, ALU.mult),
                  ["dpim", "scsel"], ["w9im"])
            else:
                E("vector", lambda v, vr=vr, c=c: v.scalar_tensor_tensor(
                      w9re_v, vr, scsel[:, c:c+1], w9re_v, ALU.mult, ALU.add),
                  ["dpre", "scsel", "w9re"], ["w9re"])
                E("vector", lambda v, vi=vi, c=c: v.scalar_tensor_tensor(
                      w9im_v, vi, scsel[:, c:c+1], w9im_v, ALU.mult, ALU.add),
                  ["dpim", "scsel", "w9im"], ["w9im"])

    load_w1(0)
    load_w2(0)
    load_phz(0)
    load_ln_vectors(0)
    load_b_vectors(0)
    gen_tables_a()
    gen_tables_b()
    if "pcoltap" in taps:
        E("sync", lambda s: s.dma_start(out=tap_d["pcoltap"][:, :], in_=pcol[:]),
          ["pcol"], dma=True)
    if "tabs" in taps:
        E("sync", lambda s: s.dma_start(out=tap_d["tabs"][:, :], in_=tabbuf[:]),
          ["ctabA", "ctabB", "stabA", "stabB", "p0tab"], dma=True)

    def emit_front(js):
        for j in js:
            ln_stats(h[j][:], ["h%d" % j], j, 1)
        finalize_batch(js[0], len(js), "vector")
        for j in js:
            q = j % 4
            y16 = xbuf16[:, q*1024:q*1024 + 512]
            rq = "xq%d" % q
            tmpx = tmp512a if j % 2 == 0 else tmp512b
            rt = "tmp512a" if j % 2 == 0 else "tmp512b"
            tmp16 = tmpx[:].bitcast(BF16)[:, 0:DIM]
            E("scalar", lambda s, j=j, tmp16=tmp16: s.activation(
                  out=tmp16, in_=h[j][:], func=AF.Identity,
                  scale=rstd8[:, j:j+1], bias=nmr8[:, j:j+1]),
              ["h%d" % j, "rstd8", "nmr8"], [rt])
            E("vector", lambda v, tmp16=tmp16: v.tensor_tensor(tmp16, tmp16, l1wb16[:], ALU.mult),
              [rt, "l1wb"], [rt])
            E("gpsimd", lambda v, tmp16=tmp16: v.tensor_tensor(tmp16, tmp16, l1bb16[:], ALU.add),
              [rt, "l1bb"], [rt])
            E("scalar", lambda s, y16=y16, tmp16=tmp16: s.activation(out=y16, in_=tmp16, func=AF.Silu),
              [rt], [rq])
            E("sync", lambda s, j=j, y16=y16: s.dma_start(
                  out=yT[:].rearrange("p (g t) -> p g t", t=1024)[:, :, j*128:(j+1)*128],
                  in_=y16, transpose=True),
              [rq], ["yTa" if j < 4 else "yTb"], dma=True)

    emit_front(list(range(NT)))

    for k in range(DEPTH):
        hreg = ["h%d" % j for j in range(NT)]
        if "ytap" in taps and k == 0:
            E("gpsimd", lambda s: s.dma_start(out=tap_d["ytap"][:, :], in_=yT[:]),
              ["yTa", "yTb"], dma=True)
        for g in range(G4):
            ug, vg = u2[g], v2[g]
            ru, rv = UVR[g]
            ytg = yT[:, g*1024:(g+1)*1024]
            E("vector", lambda v, g=g, ug=ug, ytg=ytg: v.tensor_tensor(
                  ug, c3[:, g, :], ytg, ALU.mult), ["ctabA" if g < 3 else "ctabB", "yTa", "yTb"], [ru])
            E("gpsimd", lambda v, g=g, vg=vg, ytg=ytg: v.tensor_tensor(
                  vg, s3[:, g, :], ytg, ALU.mult), ["stabA" if g < 3 else "stabB", "yTa", "yTb"], [rv])
            p0rev = p3[:, g, ::-1]
            E("vector", lambda v, ug=ug, p0rev=p0rev, g=g: v.scalar_tensor_tensor(
                  convT[:, 0:1024], ug, 1.0, p0rev, ALU.mult, ALU.mult,
                  accum_out=fcol[:, 2*g:2*g+1]),
              [ru, "p0tab"], ["cTbuf", "fcol"])
            E("vector", lambda v, vg=vg, p0rev=p0rev, g=g: v.scalar_tensor_tensor(
                  convT[:, 0:1024], vg, 1.0, p0rev, ALU.mult, ALU.mult,
                  accum_out=fcol[:, 2*g+1:2*g+2]),
              [rv, "p0tab"], ["cTbuf", "fcol"])

        c1023v = c3[:, :, 1023]
        s1023v = s3[:, :, 1023]
        ucols = fcol[:, 0:8:2]
        vcols = fcol[:, 1:8:2]
        E("vector", lambda v: v.tensor_tensor(gcol[:, 2*G4:3*G4], c1023v, ucols, ALU.mult),
          ["ctabA", "ctabB", "fcol"], ["gcolT"])
        E("vector", lambda v: v.tensor_tensor(gcol[:, 3*G4:4*G4], s1023v, vcols, ALU.mult),
          ["stabA", "stabB", "fcol"], ["gcolT"])
        E("vector", lambda v: v.tensor_tensor(gcol[:, 2*G4:3*G4], gcol[:, 2*G4:3*G4], gcol[:, 3*G4:4*G4], ALU.add),
          ["gcolT"], ["gcolT"])
        E("vector", lambda v: v.tensor_tensor(gcol[:, 4*G4:5*G4], s1023v, ucols, ALU.mult),
          ["stabA", "stabB", "fcol"], ["gcolT"])
        E("vector", lambda v: v.tensor_tensor(gcol[:, 5*G4:6*G4], c1023v, vcols, ALU.mult),
          ["ctabA", "ctabB", "fcol"], ["gcolT"])
        E("vector", lambda v: v.tensor_tensor(gcol[:, 4*G4:5*G4], gcol[:, 4*G4:5*G4], gcol[:, 5*G4:6*G4], ALU.subtract),
          ["gcolT"], ["gcolT"])
        E("vector", lambda v: v.tensor_copy(gstage[:, 0:8:2], gcol[:, 2*G4:3*G4]),
          ["gcolT"], ["gstage"])
        E("vector", lambda v: v.tensor_copy(gstage[:, 1:8:2], gcol[:, 4*G4:5*G4]),
          ["gcolT"], ["gstage"])
        E("scalar", lambda s: s.activation(out=gstage2[:], in_=gstage[:], func=AF.Identity),
          ["gstage"], ["gstage2"])
        E("gpsimd", lambda g2, k=k: g2.dma_start(
              out=cc_in[k].rearrange("a (p q) -> (a p) q", p=128), in_=gstage2[:]),
          ["gstage2"], ["ccin%d" % k], dma=True)
        E("gpsimd", lambda g2, k=k: g2.collective_compute(
              "AllGather", ALU.bypass, replica_groups=[list(range(NCORES))],
              ins=[cc_in[k].opt()], outs=[cc_out[k].opt()]),
          ["ccin%d" % k], ["ccout%d" % k], cc=True)
        E("gpsimd", lambda v: v.tensor_copy(gath[:, 0:8:2], pc(LR)), ["pcol"], ["gath"])
        E("gpsimd", lambda v: v.tensor_copy(gath[:, 1:8:2], pc(LI)), ["pcol"], ["gath"])
        E("gpsimd", lambda g2, k=k: g2.dma_start(
              out=gath[:, 8:72].rearrange("p (j q) -> p j q", q=8),
              in_=cc_out[k].rearrange("j (p q) -> p j q", p=128)),
          ["ccout%d" % k], ["gath"], dma=True)

        rho_b = pc(RH).rearrange("p (g o) -> p g o", o=1)
        for g in range(G4):
            ug, vg = u2[g], v2[g]
            ru, rv = UVR[g]
            rb1 = rho_b[:, g, :].broadcast_to((128, 1024))
            E("vector", lambda v, ug=ug, rb1=rb1: v.tensor_tensor_scan(
                  out=ug, data0=rb1, data1=ug, initial=0.0, op0=ALU.mult, op1=ALU.add),
              [ru, "pcol"], [ru])
            E("vector", lambda v, vg=vg, rb1=rb1: v.tensor_tensor_scan(
                  out=vg, data0=rb1, data1=vg, initial=0.0, op0=ALU.mult, op1=ALU.add),
              [rv, "pcol"], [rv])

        if "uv0" in taps and k == 0:
            E("sync", lambda s: s.dma_start(out=tap_d["uv0"][:, 0:1024], in_=u2[0]),
              [SCN[0]], dma=True)
            E("sync", lambda s: s.dma_start(out=tap_d["uv0"][:, 1024:2048], in_=v2[0]),
              [SCN[1]], dma=True)
        w9re_v = w9re[:].rearrange("p (j g) -> p j g", g=G4)
        w9im_v = w9im[:].rearrange("p (j g) -> p j g", g=G4)
        gview = gath[:].rearrange("p (j g r) -> p j g r", g=G4, r=2)
        gre_v = gview[:, :, :, 0]
        gim_v = gview[:, :, :, 1]
        t9a_w = t9a[:].rearrange("p (g j) -> p j g", g=G4)
        t9b_w = t9b[:].rearrange("p (g j) -> p j g", g=G4)
        t9a_r = t9a[:].rearrange("p (g j) -> p g j", j=9)
        t9b_r = t9b[:].rearrange("p (g j) -> p g j", j=9)
        E("vector", lambda v: v.tensor_tensor(t9a_w, w9re_v, gre_v, ALU.mult),
          ["w9re", "gath"], ["t9a"])
        E("gpsimd", lambda v: v.tensor_tensor(t9b_w, w9im_v, gim_v, ALU.mult),
          ["w9im", "gath"], ["t9b"])
        E("vector", lambda v: v.tensor_tensor(t9a_r, t9a_r, t9b_r, ALU.subtract),
          ["t9a", "t9b"], ["t9a"])
        E("vector", lambda v: v.reduce_sum(gcol[:, 0:G4], t9a_r, axis=mybir.AxisListType.X),
          ["t9a"], ["gcolG"])
        E("vector", lambda v: v.tensor_tensor(t9a_w, w9re_v, gim_v, ALU.mult),
          ["w9re", "gath"], ["t9a"])
        E("gpsimd", lambda v: v.tensor_tensor(t9b_w, w9im_v, gre_v, ALU.mult),
          ["w9im", "gath"], ["t9b"])
        E("vector", lambda v: v.tensor_tensor(t9a_r, t9a_r, t9b_r, ALU.add),
          ["t9a", "t9b"], ["t9a"])
        E("vector", lambda v: v.reduce_sum(gcol[:, G4:2*G4], t9a_r, axis=mybir.AxisListType.X),
          ["t9a"], ["gcolG"])
        E("vector", lambda v: v.tensor_tensor(gcol[:, 2*G4:3*G4], pc(CO), gcol[:, 0:G4], ALU.mult),
          ["pcol", "gcolG"], ["gcolT"])
        E("vector", lambda v: v.tensor_tensor(gcol[:, 3*G4:4*G4], pc(SI), gcol[:, G4:2*G4], ALU.mult),
          ["pcol", "gcolG"], ["gcolT"])
        E("vector", lambda v: v.tensor_tensor(gcol[:, 2*G4:3*G4], gcol[:, 2*G4:3*G4], gcol[:, 3*G4:4*G4], ALU.subtract),
          ["gcolT"], ["gcolT"])
        E("vector", lambda v: v.tensor_tensor(gcol[:, 4*G4:5*G4], pc(SI), gcol[:, 0:G4], ALU.mult),
          ["pcol", "gcolG"], ["gcolT"])
        E("vector", lambda v: v.tensor_tensor(gcol[:, 5*G4:6*G4], pc(CO), gcol[:, G4:2*G4], ALU.mult),
          ["pcol", "gcolG"], ["gcolT"])
        E("vector", lambda v: v.tensor_tensor(gcol[:, 4*G4:5*G4], gcol[:, 4*G4:5*G4], gcol[:, 5*G4:6*G4], ALU.add),
          ["gcolT"], ["gcolT"])
        E("vector", lambda v: v.tensor_tensor(gcol[:, 6*G4:7*G4], pc(RH), gcol[:, 2*G4:3*G4], ALU.mult),
          ["pcol", "gcolT"], ["gcolC"])
        E("vector", lambda v: v.scalar_tensor_tensor(gcol[:, 7*G4:8*G4], pc(RH), -1.0, gcol[:, 4*G4:5*G4], ALU.mult, ALU.mult),
          ["pcol", "gcolT"], ["gcolC"])

        if "fin" in taps and k == 0:
            E("sync", lambda s: s.dma_start(out=tap_d["fin"][:, 0:16], in_=fcol[:]),
              ["fcol"], dma=True)
            E("sync", lambda s: s.dma_start(out=tap_d["fin"][:, 16:48], in_=gcol[:]),
              ["gcolC", "gcolG", "gcolT"], dma=True)
            E("sync", lambda s: s.dma_start(out=tap_d["fin"][:, 48:120], in_=gath[:]),
              ["gath"], dma=True)
        for g in range(G4):
            ug, vg = u2[g], v2[g]
            ru, rv = UVR[g]
            cv = cv16[g]
            rcv = "xq%d" % g
            E("vector", lambda v, g=g, ug=ug: v.scalar_tensor_tensor(
                  ug, p3[:, g, :], gcol[:, 6*G4+g:6*G4+g+1], ug, ALU.mult, ALU.add),
              [ru, "p0tab", "gcolC"], [ru])
            E("vector", lambda v, g=g, vg=vg: v.scalar_tensor_tensor(
                  vg, p3[:, g, :], gcol[:, 7*G4+g:7*G4+g+1], vg, ALU.mult, ALU.add),
              [rv, "p0tab", "gcolC"], [rv])
            dme = "vector" if g < 3 else "gpsimd"
            E(dme, lambda v, g=g, ug=ug: v.tensor_tensor(ug, ug, c3[:, g, :], ALU.mult),
              [ru, "ctabA" if g < 3 else "ctabB"], [ru])
            E(dme, lambda v, g=g, vg=vg: v.tensor_tensor(vg, vg, s3[:, g, :], ALU.mult),
              [rv, "stabA" if g < 3 else "stabB"], [rv])
            E(dme, lambda v, ug=ug, vg=vg: v.tensor_tensor(ug, ug, vg, ALU.add),
              [ru, rv], [ru])
            E("scalar", lambda s, g=g, ug=ug, cv=cv: s.activation(
                  out=cv, in_=ug, func=AF.Identity, scale=pc(SC)[:, g:g+1]),
              [ru, "pcol"], [rcv])
        for g in range(G4):
            cv = cv16[g]
            E("sync", lambda s, g=g, cv=cv: s.dma_start(
                  out=convT[:].rearrange("p (jj d) -> p jj d", d=DIM)[:, :, g*128:(g+1)*128],
                  in_=cv, transpose=True),
              ["xq%d" % g], ["cTbuf"], dma=True)

        if "convtap" in taps and k == 0:
            E("gpsimd", lambda s: s.dma_start(out=tap_d["convtap"][:, :], in_=convT[:]),
              ["cTbuf"], dma=True)

        TORD = [4, 5, 6, 7, 0, 1, 2, 3]
        for j in TORD:
            E("gpsimd", lambda v, j=j: v.tensor_tensor(
                  h[j][:], h[j][:], convT[:, j*DIM:(j+1)*DIM], ALU.add),
              ["h%d" % j, "cTbuf"], ["h%d" % j])
        for j in TORD:
            ln_stats(h[j][:], ["h%d" % j], j, 1)
        finalize_batch(4, 4, "vector")
        finalize_batch(0, 4, "vector")
        for j in TORD:
            q = j % 4
            z16 = xbuf16[:, q*1024:q*1024 + 512]
            rq = "xq%d" % q
            tmpx = tmp512a if j % 2 == 0 else tmp512b
            rt = "tmp512a" if j % 2 == 0 else "tmp512b"
            tmp16 = tmpx[:].bitcast(BF16)[:, 0:DIM]
            E("scalar", lambda s, j=j, tmp16=tmp16: s.activation(
                  out=tmp16, in_=h[j][:], func=AF.Identity,
                  scale=rstd8[:, j:j+1], bias=nmr8[:, j:j+1]),
              ["h%d" % j, "rstd8", "nmr8"], [rt])
            E("vector", lambda v, tmp16=tmp16: v.tensor_tensor(tmp16, tmp16, l2wb16[:], ALU.mult),
              [rt, "l2wb"], [rt])
            E("gpsimd", lambda v, tmp16=tmp16: v.tensor_tensor(tmp16, tmp16, l2bb16[:], ALU.add),
              [rt, "l2bb"], [rt])
            E("sync", lambda s, j=j, tmp16=tmp16: s.dma_start(
                  out=yT[:].rearrange("p (g t) -> p g t", t=1024)[:, :, j*128:(j+1)*128],
                  in_=tmp16, transpose=True),
              [rt], ["yTa" if j < 4 else "yTb"], dma=True)

        if k + 1 < DEPTH:
            load_ln_vectors(k + 1)

        for idx, st_ in enumerate((1, 0)):
            if idx == 0 and k + 1 < DEPTH:
                load_phz(k + 1)
                gen_tables_a()
            if idx == 1 and k + 1 < DEPTH:
                gen_tables_b()
            yreg = "yTb" if st_ == 1 else "yTa"
            for half in range(2):
                for i in range(8):
                    fi = half * 8 + i
                    for g in range(G4):
                        E("tensor", lambda t, fi=fi, g=g, st_=st_, i=i: t.matmul(
                              ps[i][:],
                              w1T16[:, g*DFF + fi*128: g*DFF + (fi+1)*128],
                              yT[:, g*1024 + st_*512: g*1024 + (st_+1)*512],
                              start=(g == 0), stop=(g == G4 - 1)),
                          ["wbufA", yreg], ["ps%d" % i], no_selfdep=True)
                for i in range(8):
                    fi = half * 8 + i
                    E("scalar", lambda s, fi=fi, i=i: s.activation(
                          out=z1T[:, fi*512:(fi+1)*512], in_=ps[i][:], func=AF.Silu,
                          bias=b1c[:, fi:fi+1]),
                      ["ps%d" % i, "b1c"], [Z1R[fi // 4]])
            for jj in range(4):
                j = st_ * 4 + jj
                pj = ps[jj % 2]
                rp = "ps%d" % (jj % 2)
                for g in range(FG):
                    E("tensor", lambda t, g=g, jj=jj, pj=pj: t.matmul(
                          pj[:],
                          z1T[:, g*512 + jj*128: g*512 + (jj+1)*128],
                          w2T16[:, g*DIM:(g+1)*DIM],
                          start=(g == 0), stop=(g == FG - 1)),
                      [Z1R[g // 4], "wbufB"], [rp], no_selfdep=True)
                E("scalar", lambda s, pj=pj: s.activation(
                      out=tmp512c[:], in_=pj[:], func=AF.Identity),
                  [rp], ["tmp512c"])
                E("vector", lambda v, j=j: v.tensor_tensor(h[j][:], h[j][:], tmp512c[:], ALU.add),
                  ["h%d" % j, "tmp512c"], ["h%d" % j])
                E("vector", lambda v, j=j: v.tensor_tensor(h[j][:], h[j][:], b2b[:], ALU.add),
                  ["h%d" % j, "b2b"], ["h%d" % j])
            if idx == 0 and k + 1 < DEPTH:
                emit_front([4, 5, 6, 7])
        if k + 1 < DEPTH:
            emit_front([0, 1, 2, 3])

        if ("h%d" % (k + 1)) in taps:
            for j in range(NT):
                E("sync", lambda s, j=j, k=k: s.dma_start(
                      out=tap_d["h%d" % (k+1)][j*128:(j+1)*128, :], in_=h[j][:]),
                  ["h%d" % j], dma=True)

        if k + 1 < DEPTH:
            load_w1(k + 1)
            load_w2(k + 1)
            load_b_vectors(k + 1)

    E("gpsimd", lambda g: g.dma_start(
          out=woutT16[:].rearrange("p (g v) -> p g v", v=D),
          in_=woutT_d.rearrange("(g p) v -> p g v", p=128)), writes=WBUF, dma=True)
    E("gpsimd", lambda g: g.dma_start(
          out=awrA16[:].rearrange("p (g n) -> p g n", n=128),
          in_=awrA_d.rearrange("(g p) n -> p g n", p=128)), writes=["p0tab"], dma=True)
    E("gpsimd", lambda g: g.dma_start(
          out=awrW16[:].rearrange("p (g n) -> p g n", n=H),
          in_=awrW_d.rearrange("(g p) n -> p g n", p=128)), writes=["awrW16"], dma=True)
    E("gpsimd", lambda g: g.dma_start(out=b16c[:], in_=bflat_d[:]),
      writes=["p0tab"], dma=True)
    E("gpsimd", lambda g: g.dma_start(out=boutb[:], in_=bout_d.partition_broadcast(128)),
      writes=SCN, dma=True)
    E("gpsimd", lambda g: g.dma_start(out=l1wb16[:], in_=lnlw_d.partition_broadcast(128)),
      writes=["l1wb"], dma=True)
    E("gpsimd", lambda g: g.dma_start(out=l1bb16[:], in_=lnlb_d.partition_broadcast(128)),
      writes=["l1bb"], dma=True)


    em.full_barrier()
    for j in range(NT):
        ln_stats(h[j][:], ["h%d" % j], j, 1)
    finalize_batch(0, NT, "vector")
    if "lnldbg" in taps:
        E("sync", lambda s: s.dma_start(out=tap_d["lnldbg"][:, 0:512], in_=h[0][:]),
          ["h0"], dma=True)
        E("vector", lambda v: v.tensor_copy(tmp512b[:, 0:8], rstd8[:]), ["rstd8"], ["tmp512b"])
        E("vector", lambda v: v.tensor_copy(tmp512b[:, 8:16], nmr8[:]), ["nmr8"], ["tmp512b"])
        E("vector", lambda v: v.tensor_copy(tmp512b[:, 16:32], mvst[:]), ["mvst"], ["tmp512b"])
        E("sync", lambda s: s.dma_start(out=tap_d["lnldbg"][:, 512:544], in_=tmp512b[:, 0:32]),
          ["tmp512b"], dma=True)
    for j in range(NT):
        q = j % 4
        hf16 = xbuf16[:, q*1024:q*1024 + 512]
        rq = "xq%d" % q
        tmpx = tmp512a if j % 2 == 0 else tmp512b
        rt = "tmp512a" if j % 2 == 0 else "tmp512b"
        tmp16 = tmpx[:].bitcast(BF16)[:, 0:DIM]
        E("scalar", lambda s, j=j, tmp16=tmp16: s.activation(
              out=tmp16, in_=h[j][:], func=AF.Identity,
              scale=rstd8[:, j:j+1], bias=nmr8[:, j:j+1]),
          ["h%d" % j, "rstd8", "nmr8"], [rt])
        E("vector", lambda v, tmp16=tmp16: v.tensor_tensor(tmp16, tmp16, l1wb16[:], ALU.mult),
          [rt, "l1wb"], [rt])
        E("gpsimd", lambda v, tmp16=tmp16: v.tensor_tensor(tmp16, tmp16, l1bb16[:], ALU.add),
          [rt, "l1bb"], [rt])
        E("sync", lambda s, j=j, tmp16=tmp16: s.dma_start(
              out=hfT[:].rearrange("p (g t) -> p g t", t=1024)[:, :, j*128:(j+1)*128],
              in_=tmp16, transpose=True),
          [rt], ["yTa" if j < 4 else "yTb"], dma=True)
        if "hf" in taps:
            E("vector", lambda v, hf16=hf16: v.tensor_copy(tmp512a[:], hf16),
              ["xq%d" % q], ["tmp512a"])
            E("sync", lambda s, j=j: s.dma_start(
                  out=tap_d["hf"][j*128:(j+1)*128, :], in_=tmp512a[:]),
              ["tmp512a"], dma=True)

    BIGR = [["ctabA", "ctabB"], ["stabA", "stabB"]]
    for b in range(4):
        tiles = [2*b, 2*b + 1]
        for ji, j in enumerate(tiles):
            bigj = big[ji]
            rbig = BIGR[ji]
            for vc in range(8):
                pv = ps[vc]
                rp = "ps%d" % vc
                for g in range(G4):
                    E("tensor", lambda t, g=g, j=j, vc=vc, pv=pv: t.matmul(
                          pv[:],
                          hfT[:, g*1024 + j*128: g*1024 + (j+1)*128],
                          woutT16[:, g*D + vc*512: g*D + (vc+1)*512],
                          start=(g == 0), stop=(g == G4 - 1)),
                      ["yTa", "yTb"] + WBUF, [rp], no_selfdep=True)
                E("vector", lambda v, vc=vc, pv=pv, bigj=bigj: v.scalar_tensor_tensor(
                      bigj[:, vc*512:(vc+1)*512], pv[:], 1.0, boutb[:, vc*512:(vc+1)*512],
                      ALU.mult, ALU.add, accum_out=statc[:, vc:vc+1]),
                  [rp] + SCN, rbig + ["statc"])
            E("scalar", lambda s, bigj=bigj: s.activation(
                  out=xbuf16[:], in_=bigj[:], func=AF.Square,
                  accum_out=stat2[:, 0:1]),
              rbig, XQ + ["stat2"])
            if "attn" in taps:
                E("sync", lambda s, j=j, bigj=bigj: s.dma_start(
                      out=tap_d["attn"][j*128:(j+1)*128, :], in_=bigj[:]),
                  rbig, dma=True)
            stats_from_sums(j, D)
        finalize_batch(2*b, 2, "vector")
        for ji, j in enumerate(tiles):
            bigj = big[ji]
            rbig = BIGR[ji]
            xs = xstage[ji]
            xsregs = XS0 if ji == 0 else ["xst1"]
            E("sync", lambda s, j=j, xs=xs: s.dma_start(out=xs[:], in_=x_d[j*128:(j+1)*128, :]),
              writes=xsregs, dma=True)
            E("scalar", lambda s, j=j, xs=xs: s.activation(
                  out=xs[:], in_=xs[:], func=AF.Identity, bias=nmr8[:, j:j+1]),
              xsregs + ["nmr8"], xsregs)
            E("vector", lambda v, j=j, bigj=bigj, xs=xs: v.scalar_tensor_tensor(
                  bigj[:], bigj[:], rstd8[:, j:j+1], xs[:], ALU.mult, ALU.add,
                  accum_out=statc[:, 0:1]),
              rbig + xsregs + ["rstd8"], rbig + ["statc"])
            E("scalar", lambda s, bigj=bigj: s.activation(
                  out=xbuf16[:], in_=bigj[:], func=AF.Square,
                  accum_out=stat2[:, 0:1]),
              rbig, XQ + ["stat2"])
            if "x1" in taps:
                E("sync", lambda s, j=j, bigj=bigj: s.dma_start(
                      out=tap_d["x1"][j*128:(j+1)*128, :], in_=bigj[:]),
                  rbig, dma=True)
            stats_from_sums(j, D, ncol=1)
        finalize_batch(2*b, 2, "vector")
        for ji, j in enumerate(tiles):
            bigj = big[ji]
            rbig = BIGR[ji]
            E("scalar", lambda s, j=j, bigj=bigj: s.activation(
                  out=xbuf16[:], in_=bigj[:], func=AF.Identity,
                  scale=rstd8[:, j:j+1], bias=nmr8[:, j:j+1]),
              rbig + ["rstd8", "nmr8"], XQ)
            E("sync", lambda s: s.dma_start(
                  out=x1lnT[:].rearrange("p (g t) -> p g t", g=DG), in_=xbuf16[:],
                  transpose=True),
              XQ, ["cTbuf"], dma=True)
            for g in range(DG):
                E("tensor", lambda t, g=g: t.matmul(
                      ps[4][:, 0:128], x1lnT[:, g*128:(g+1)*128],
                      awrA16[:, g*128:(g+1)*128],
                      start=(g == 0), stop=(g == DG - 1)),
                  ["cTbuf", "p0tab"], ["ps4"], no_selfdep=True)
            for g in range(DG):
                E("tensor", lambda t, g=g: t.matmul(
                      ps[5][:, 0:H], x1lnT[:, g*128:(g+1)*128],
                      awrW16[:, g*H:(g+1)*H],
                      start=(g == 0), stop=(g == DG - 1)),
                  ["cTbuf", "awrW16"], ["ps5"], no_selfdep=True)
            E("vector", lambda v: v.scalar_tensor_tensor(
                  logitsb, ps[5][:, 0:H], 1.0, brb[:], ALU.mult, ALU.add),
              ["ps5", "brb"], ["logitsb"])
            E("vector", lambda v: v.reduce_max(sinvc, logitsb, axis=mybir.AxisListType.X),
              ["logitsb"], ["sinvc"])
            E("vector", lambda v: v.tensor_scalar(sinvc, sinvc, -1.0, None, ALU.mult),
              ["sinvc"], ["sinvc"])
            E("scalar", lambda s: s.activation(out=exps, in_=logitsb, func=AF.Exp,
                                               bias=sinvc),
              ["logitsb", "sinvc"], ["exps"])
            E("vector", lambda v: v.reduce_sum(sinvc, exps, axis=mybir.AxisListType.X),
              ["exps"], ["sinvc"])
            E("vector", lambda v: v.reciprocal(sinvc, sinvc), ["sinvc"], ["sinvc"])
            E("vector", lambda v: v.tensor_scalar(sinvc, sinvc, 16.0, None, ALU.mult),
              ["sinvc"], ["sinvc"])
            eb = exps.unsqueeze(2).broadcast_to((128, H, 16))
            mlf = deadbuf[:, 384:512]
            E("vector", lambda v, eb=eb: v.tensor_tensor(
                  mlf.rearrange("p (hh r) -> p hh r", r=16),
                  ps[4][:, 0:128].rearrange("p (hh r) -> p hh r", r=16), eb, ALU.mult),
              ["ps4", "exps"], ["ml16"])
            E("tensor", lambda t: t.transpose(ps[6][:, 0:128], mlf, eyef[:]),
              ["ml16", "eyef"], ["ps6"])
            E("vector", lambda v: v.tensor_copy(mlT16, ps[6][:, 0:128]),
              ["ps6"], ["mlT16"])
            if "ffdbg" in taps and b == 0 and ji == 0:
                E("vector", lambda v: v.tensor_copy(tmp512a[:, 0:128], ps[4][:, 0:128]),
                  ["ps4"], ["tmp512a"])
                E("vector", lambda v: v.tensor_copy(tmp512a[:, 128:256], deadbuf[:, 384:512]),
                  ["ml16"], ["tmp512a"])
                E("vector", lambda v: v.tensor_copy(tmp512a[:, 256:384], mlT16),
                  ["mlT16"], ["tmp512a"])
                E("vector", lambda v: v.tensor_copy(tmp512a[:, 384:416], deadbuf[:, 0:32]),
                  ["logitsb", "exps", "sinvc"], ["tmp512a"])
                E("sync", lambda s: s.dma_start(out=tap_d["ffdbg"][:, 0:448],
                                                in_=tmp512a[:, 0:448]),
                  ["tmp512a"], dma=True)
            xsregs = XS0 if ji == 0 else ["xst1"]
            for oc in range(8):
                pv = ps[oc % 2]
                rp = "ps%d" % (oc % 2)
                E("tensor", lambda t, oc=oc, pv=pv: t.matmul(
                      pv[:], mlT16, b16c[:, oc*512:(oc+1)*512], start=True, stop=True),
                  ["mlT16", "p0tab"], [rp])
                E("scalar", lambda s, oc=oc, pv=pv, ji=ji: s.activation(
                      out=xstage[ji][:, oc*512:(oc+1)*512], in_=pv[:], func=AF.Identity,
                      scale=sinvc, accum_out=statc[:, oc:oc+1]),
                  [rp, "sinvc"], xsregs + ["statc"])
            if "ffn" in taps:
                E("sync", lambda s, j=j, ji=ji: s.dma_start(
                      out=tap_d["ffn"][j*128:(j+1)*128, :], in_=xstage[ji][:]),
                  xsregs, dma=True)
            E("scalar", lambda s, ji=ji: s.activation(
                  out=xbuf16[:], in_=xstage[ji][:], func=AF.Square,
                  accum_out=stat2[:, 0:1]),
              xsregs, XQ + ["stat2"])
            stats_from_sums(j, D)
        finalize_batch(2*b, 2, "vector")
        for ji, j in enumerate(tiles):
            bigj = big[ji]
            rbig = BIGR[ji]
            xsregs = XS0 if ji == 0 else ["xst1"]
            E("scalar", lambda s, j=j, ji=ji: s.activation(
                  out=xstage[ji][:], in_=xstage[ji][:], func=AF.Identity,
                  scale=rstd8[:, j:j+1], bias=nmr8[:, j:j+1]),
              xsregs + ["rstd8", "nmr8"], xsregs)
            E("gpsimd", lambda v, ji=ji, bigj=bigj: v.tensor_tensor(
                  xstage[ji][:], bigj[:], xstage[ji][:], ALU.add),
              rbig + xsregs, xsregs)
            E("sync", lambda s, j=j, ji=ji: s.dma_start(
                  out=out_d[j*128:(j+1)*128, :], in_=xstage[ji][:]),
              xsregs, dma=True)

    em.barrier_wait("vector", em.milestone("sdma_sync"))
    em.barrier_wait("vector", em.milestone("sdma_gp"))
    em.barrier_wait("vector", em.milestone("sdma_act"))

    sems = {s: es.enter_context(nc.semaphore(s)) for s in SEMS}
    with es:
        block = nc.Block()
        with block:
            em.replay(block, sems)
    return nc, tap_d



def _prep_inputs(inputs):
    g = {}
    x = np.ascontiguousarray(np.asarray(inputs["L0_attn_x"], np.float32).reshape(8192, D))
    Win = np.asarray(inputs["L0_attn_Win"], np.float32)
    g["winT"] = np.ascontiguousarray(Win.T)
    g["bin"] = np.asarray(inputs["L0_attn_bin"], np.float32).reshape(1, DIM)
    W1 = np.asarray(inputs["L0_attn_W1"], np.float32)
    g["w1T"] = np.ascontiguousarray(W1.transpose(0, 2, 1))
    W2 = np.asarray(inputs["L0_attn_W2"], np.float32)
    g["w2T"] = np.ascontiguousarray(W2.transpose(0, 2, 1))
    b1 = np.asarray(inputs["L0_attn_b1"], np.float32)
    g["b1c"] = np.ascontiguousarray(b1.reshape(DEPTH, FG, 128).transpose(0, 2, 1))
    g["b2"] = np.asarray(inputs["L0_attn_b2"], np.float32).reshape(DEPTH, 1, DIM)
    for nm, key in (("ln1w", "L0_attn_ln1_w"), ("ln1b", "L0_attn_ln1_b"),
                    ("ln2w", "L0_attn_ln2_w"), ("ln2b", "L0_attn_ln2_b")):
        g[nm] = np.asarray(inputs[key], np.float32).reshape(DEPTH, 1, DIM)
    for nm, key in (("phzre", "L0_attn_phz_re"), ("phzim", "L0_attn_phz_im"),
                    ("lcre", "L0_attn_lc_re"), ("lcim", "L0_attn_lc_im"),
                    ("sclc", "L0_attn_sc_lin")):
        v = np.asarray(inputs[key], np.float32)
        g[nm] = np.ascontiguousarray(v.reshape(DEPTH, G4, 128).transpose(0, 2, 1))
    g["lnlw"] = np.asarray(inputs["L0_attn_lnl_w"], np.float32).reshape(1, DIM)
    g["lnlb"] = np.asarray(inputs["L0_attn_lnl_b"], np.float32).reshape(1, DIM)
    Wout = np.asarray(inputs["L0_attn_Wout"], np.float32)
    g["woutT"] = np.ascontiguousarray(Wout.T)
    g["bout"] = np.asarray(inputs["L0_attn_bout"], np.float32).reshape(1, D)
    A = np.asarray(inputs["L0_ffn_A"], np.float32)
    Wr = np.asarray(inputs["L0_ffn_Wr"], np.float32)
    g["awrA"] = np.ascontiguousarray(A.reshape(128, D).T)
    g["awrW"] = np.ascontiguousarray(Wr.T)
    Bm = np.asarray(inputs["L0_ffn_B"], np.float32)
    g["bflat"] = np.ascontiguousarray(Bm.transpose(0, 2, 1).reshape(128, D))
    g["br"] = np.asarray(inputs["L0_ffn_br"], np.float32).reshape(1, H)
    g["eye"] = np.eye(128, dtype=np.float32)
    return x, g


def kernel(**inputs):
    x, g = _prep_inputs(inputs)
    nc, _ = build_program()
    in_maps = []
    for c in range(NCORES):
        m = dict(g)
        m["x"] = np.ascontiguousarray(x[c*T:(c+1)*T, :])
        m["csel"] = np.eye(NCORES, dtype=np.float32)[c:c+1, :]
        in_maps.append(m)
    res = run_bass_kernel_spmd(nc, in_maps, core_ids=list(range(NCORES)))
    out = np.concatenate([res.results[c]["out"] for c in range(NCORES)], axis=0)
    return out.reshape(1, 8192, D)
